# version 01
# baseline (speedup 1.0000x reference)
import numpy as np
import concourse.bass as bass
import concourse.mybir as mybir
from concourse.ap import AP
from concourse.bass_utils import run_bass_kernel_spmd

F32 = mybir.dt.float32
BF16 = mybir.dt.bfloat16
I32 = mybir.dt.int32
AF = mybir.ActivationFunctionType
ALU = mybir.AluOpType
AX = mybir.AxisListType

ENGINES = ("sync", "tensor", "vector", "scalar", "gpsimd")
PAIRS = [[0, 1], [2, 3], [4, 5], [6, 7]]

B, S, D = 4, 4096, 1024
T = 2048
NBLK = 4
BW = 512
NC8 = 8
EPS_LN = 1e-5
MAGIC = 12582912.0


class Buf:
    __slots__ = ("name", "w", "rs")

    def __init__(self, name):
        self.name = name
        self.w = None
        self.rs = []


class Prog:
    def __init__(self, nc):
        self.nc = nc
        self.items = {e: [] for e in ENGINES}
        self.counts = {}
        self.tiles = {}
        self.sb_bytes = 0
        self.psum_banks = []
        self._uid = 0
        self.serial = False
        self.serial_engines = set()
        self.serial_tags = set()
        self.dma_idx = {}

    def _bump(self, key, amt):
        v = self.counts.get(key, 0) + amt
        self.counts[key] = v
        return (key, v)

    def emit(self, eng, fn, reads=(), writes=(), kind="c", accum=False, tag=None):
        if self.serial or eng in self.serial_engines or (tag is not None and tag in self.serial_tags):
            deps = [(k, v) for k, v in self.counts.items()]
        else:
            deps = []
            for b in reads:
                if b.w is not None:
                    deps.append(b.w)
            for b in writes:
                if b.w is not None:
                    deps.append(b.w)
                deps.extend(b.rs)
        if kind == "c":
            tok = self._bump(("c", eng), 1)
        elif kind == "d":
            i = self.dma_idx.get(eng, 0)
            self.dma_idx[eng] = i + 1
            lane = ("d", eng, i % 8)
            if i >= 8:
                deps.append((lane, 16 * (i // 8)))
            tok = self._bump(lane, 16)
        elif kind == "coll":
            i = self.dma_idx.get("coll", 0)
            self.dma_idx["coll"] = i + 1
            lane = ("coll", i % 4)
            if i >= 4:
                deps.append((lane, i // 4))
            tok = self._bump(lane, 1)
        else:
            raise ValueError(kind)
        dd = {}
        for (skey, val) in deps:
            if skey == ("c", eng) and eng == "tensor":
                continue
            if dd.get(skey, 0) < val:
                dd[skey] = val
        final = list(dd.items())
        self.items[eng].append((final, fn, tok))
        t2 = tok
        for b in reads:
            b.rs.append(t2)
        for b in writes:
            b.w = t2
            b.rs = []
        return t2

    GLOBALS = ("cvec_sb", "ones_col", "ones_col16", "ones_row", "ident_sb")
    ARENA_BYTES = 200704

    def _ensure_arena(self):
        if not hasattr(self, "arena"):
            self.arena = self.nc.alloc_sbuf_tensor("ARENA", [128, self.ARENA_BYTES // 4], F32)
            self.arena_off = 0
            self.phase_tiles = {}
            self.legacy = {}

    def new_phase(self):
        self._ensure_arena()
        for (ap, buf) in self.phase_tiles.values():
            toks = ([buf.w] if buf.w else []) + buf.rs
            for (k, v) in toks:
                if self.legacy.get(k, 0) < v:
                    self.legacy[k] = v
        self.phase_tiles = {}
        self.arena_off = 0

    def tile(self, name, shape, dtype):
        self._ensure_arena()
        if name in self.GLOBALS:
            if name in self.tiles:
                return self.tiles[name]
            t = self.nc.alloc_sbuf_tensor(name, list(shape), dtype)
            buf = Buf(name)
            self.tiles[name] = (t, buf)
            return t, buf
        if name in self.phase_tiles:
            return self.phase_tiles[name]
        esz = mybir.dt.size(dtype)
        cols = int(np.prod(shape[1:]))
        nbytes = cols * esz
        off = (self.arena_off + 63) & ~63
        assert off + nbytes <= self.ARENA_BYTES, \
            f"arena overflow: {name} needs {nbytes}B at {off}"
        self.arena_off = off + nbytes
        view = self.arena.bitcast(dtype)[0:shape[0], off // esz: off // esz + cols]
        buf = Buf(name)
        buf.rs = [(k, v) for k, v in self.legacy.items()]
        self.phase_tiles[name] = (view, buf)
        return view, buf

    def psum(self, name):
        if name in self.tiles:
            return self.tiles[name]
        t = self.nc.alloc_psum_tensor(name, [128, 512], F32)
        buf = Buf(name)
        self.tiles[name] = (t, buf)
        return t, buf

    def dbuf(self, name):
        key = "dram:" + name
        if key in self.tiles:
            return self.tiles[key][1]
        buf = Buf(key)
        self.tiles[key] = (None, buf)
        return buf

    def finalize(self):
        nc = self.nc
        semkeys = sorted(self.counts.keys(), key=str)
        ctxs = []
        sems = {}
        for k in semkeys:
            cm = nc.semaphore("sem_" + "_".join(map(str, k)))
            s = cm.__enter__()
            ctxs.append(cm)
            sems[k] = s
        blk_cm = nc.Block()
        block = blk_cm.__enter__()

        def mk_run(eng_name, final_wait_all=False):
            def run(e):
                watermark = {}
                for waits, fn, tok in self.items[eng_name]:
                    for (skey, val) in waits:
                        if watermark.get(skey, 0) >= val:
                            continue
                        e.wait_ge(sems[skey], val)
                        watermark[skey] = val
                    ins = fn(e)
                    skey, val = tok
                    ins.then_inc(sems[skey], 16 if skey[0] == "d" else 1)
                if final_wait_all:
                    for k in semkeys:
                        e.wait_ge(sems[k], self.counts[k])
            return run

        block.sync(mk_run("sync"))
        block.tensor(mk_run("tensor"))
        block.vector(mk_run("vector"))
        block.scalar(mk_run("scalar"))
        block.gpsimd(mk_run("gpsimd", final_wait_all=True))
        blk_cm.__exit__(None, None, None)
        for cm in reversed(ctxs):
            cm.__exit__(None, None, None)


def ceil_div(a, b):
    return (a + b - 1) // b


class MK:

    def __init__(self, dump=()):
        import os as _os
        nc = bass.Bass("TRN2", target_bir_lowering=False, debug=False,
                       detect_race_conditions=_os.environ.get("NORACE", "0") != "1")
        self.nc = nc
        self.p = Prog(nc)
        self.dump = set(dump)
        self.inputs = {}
        self.dram = {}
        self.consts = []
        self._cvec = None
        self._coll_dumps = []

    def cap(self, value, parts=128):
        value = float(value)
        if value not in self.consts:
            self.consts.append(value)
        i = self.consts.index(value)
        t, b = self.p.tile("cvec_sb", [128, 64], F32)
        return t[0:parts, i:i + 1]

    def cvec_buf(self):
        return self.p.tile("cvec_sb", [128, 64], F32)[1]

    def inp(self, name, shape, dtype=F32):
        h = self.nc.dram_tensor(name, list(shape), dtype, kind="ExternalInput")
        self.dram[name] = (h, self.p.dbuf(name))
        return h

    def scratch(self, name, shape, dtype, coll=False):
        kind = "Internal" if (coll or name not in self.dump) else "ExternalOutput"
        h = self.nc.dram_tensor(name, list(shape), dtype, kind=kind)
        self.dram[name] = (h, self.p.dbuf(name))
        if coll and name in self.dump:
            self._coll_dumps.append((name, shape, dtype))
        return h

    def flush_dumps(self):
        for (name, shape, dtype) in self._coll_dumps:
            h, b = self.d(name)
            oh = self.nc.dram_tensor(name + "_dmp", list(shape), dtype, kind="ExternalOutput")
            ob = self.p.dbuf(name + "_dmp")
            self.dram[name + "_dmp"] = (oh, ob)
            self.p.emit("gpsimd", lambda e, oh=oh, h=h: e.dma_start(out=oh.ap(), in_=h.ap()),
                        reads=[b], writes=[ob], kind="d")

    def out(self, name, shape, dtype):
        h = self.nc.dram_tensor(name, list(shape), dtype, kind="ExternalOutput")
        self.dram[name] = (h, self.p.dbuf(name))
        return h

    def d(self, name):
        return self.dram[name]

    def load(self, dst_ap, src_ap, wbuf, rbuf=None, eng="sync"):
        rd = [rbuf] if rbuf is not None else []
        return self.p.emit(eng, lambda e: e.dma_start(out=dst_ap, in_=src_ap),
                           reads=rd, writes=[wbuf], kind="d")

    def store(self, dst_ap, src_ap, rbuf, wbuf=None, eng="gpsimd"):
        wr = [wbuf] if wbuf is not None else []
        return self.p.emit(eng, lambda e: e.dma_start(out=dst_ap, in_=src_ap),
                           reads=[rbuf], writes=wr, kind="d")

    def emit_ln(self, x_name, xn32_name, xn16_name):
        p, nc = self.p, self.nc
        x_h, x_b = self.d(x_name)
        xn32_h, xn32_b = self.d(xn32_name)
        xn16_h, xn16_b = self.d(xn16_name)

        ones_col, ones_col_b = p.tile("ones_col", [128, 1], F32)
        ones_row, ones_row_b = p.tile("ones_row", [1, 128], F32)
        st, st_b = p.tile("ln_small", [1, 2048], F32)

        for nb in range(NBLK):
            xc = []
            for c in range(NC8):
                t, b = p.tile(f"ln_x{c}", [128, BW], F32)
                self.load(t[:, :], x_h[128 * c:128 * c + 128, nb * BW:(nb + 1) * BW], b, rbuf=x_b)
                xc.append((t, b))
            sq = []
            for c in range(NC8):
                t, b = p.tile(f"ln_o{c}", [128, BW], F32)
                p.emit("scalar", lambda e, t=t, s=xc[c][0]: e.activation(t[:, :], s[:, :], AF.Square),
                       reads=[xc[c][1]], writes=[b], tag="sq")
                sq.append((t, b))
            psm, psm_b = self.pb(0)
            pss, pss_b = self.pb(1)
            for c in range(NC8):
                p.emit("tensor", lambda e, c=c: e.matmul(
                    psm[0:1, :], ones_col[:, :], xc[c][0][:, :],
                    start=(c == 0), stop=(c == NC8 - 1)),
                    reads=[xc[c][1], ones_col_b], writes=[psm_b], tag="stats")
            for c in range(NC8):
                p.emit("tensor", lambda e, c=c: e.matmul(
                    pss[0:1, :], ones_col[:, :], sq[c][0][:, :],
                    start=(c == 0), stop=(c == NC8 - 1)),
                    reads=[sq[c][1], ones_col_b], writes=[pss_b], tag="stats")
            mean = st[0:1, 0:BW]
            var = st[0:1, BW:2 * BW]
            rstd = st[0:1, 2 * BW:3 * BW]
            p.emit("vector", lambda e: e.tensor_scalar(mean, psm[0:1, :], 1.0 / D, None, ALU.mult),
                   reads=[psm_b], writes=[st_b], tag="small")
            p.emit("vector", lambda e: e.tensor_tensor(rstd, mean, mean, ALU.mult),
                   reads=[st_b], writes=[st_b], tag="small")
            p.emit("vector", lambda e: e.scalar_tensor_tensor(var, pss[0:1, :], 1.0 / D, rstd, ALU.mult, ALU.subtract),
                   reads=[pss_b, st_b], writes=[st_b], tag="small")
            p.emit("scalar", lambda e: e.activation(rstd, var, AF.Ln, bias=self.cap(EPS_LN, 1)),
                   reads=[st_b, self.cvec_buf()], writes=[st_b], tag="lnexp")
            p.emit("scalar", lambda e: e.activation(rstd, rstd, AF.Exp, scale=self.cap(-0.5, 1)),
                   reads=[st_b, self.cvec_buf()], writes=[st_b], tag="lnexp")
            if "stdbg" in self.dump:
                sd_h, sd_b = self.d("stdbg")
                self.store(sd_h[nb:nb + 1, :], st[0:1, 0:3 * BW], st_b, wbuf=sd_b)
            pmb, pmb_b = self.pb(2)
            prb, prb_b = self.pb(3)
            p.emit("tensor", lambda e: e.matmul(pmb[:, :], ones_row[:, :], mean, start=True, stop=True),
                   reads=[st_b, ones_row_b], writes=[pmb_b], tag="bcast")
            p.emit("tensor", lambda e: e.matmul(prb[:, :], ones_row[:, :], rstd, start=True, stop=True),
                   reads=[st_b, ones_row_b], writes=[prb_b], tag="bcast")
            for c in range(NC8):
                t32, b32 = p.tile(f"ln_o{c}", [128, BW], F32)
                t16, b16 = p.tile(f"ln_ob{c}", [128, BW], BF16)
                p.emit("vector", lambda e, c=c, t32=t32: e.tensor_tensor(
                    t32[:, :], xc[c][0][:, :], pmb[:, :], ALU.subtract),
                    reads=[xc[c][1], pmb_b], writes=[b32], tag="tt")
                p.emit("vector", lambda e, c=c, t32=t32: e.tensor_tensor(
                    t32[:, :], t32[:, :], prb[:, :], ALU.mult),
                    reads=[prb_b], writes=[b32], tag="tt")
                p.emit("scalar", lambda e, c=c, t32=t32, t16=t16: e.activation(
                    t16[:, :], t32[:, :], AF.Identity),
                    reads=[b32], writes=[b16])
                self.store(xn32_h[128 * c:128 * c + 128, nb * BW:(nb + 1) * BW], t32[:, :], b32, wbuf=xn32_b)
                self.store(xn16_h[128 * c:128 * c + 128, nb * BW:(nb + 1) * BW], t16[:, :], b16, wbuf=xn16_b)

    def pb(self, i):
        return self.p.psum(f"pbank{i}")

    def load_weight_bf16(self, dram_name, cols, sb_name, rows=8):
        p = self.p
        w_h, w_db = self.d(dram_name)
        wsb, wsb_b = p.tile(sb_name, [128, rows * cols], BF16)
        stg = [p.tile(f"wstage{i}", [128, 3072], F32) for i in range(2)]
        for c in range(rows):
            st, st_b = stg[c % 2]
            self.load(st[:, 0:cols], w_h[128 * c:128 * (c + 1), :], st_b, rbuf=w_db)
            p.emit("vector", lambda e, st=st, c=c: e.tensor_copy(
                wsb[:, c * cols:(c + 1) * cols], st[:, 0:cols]),
                reads=[st_b], writes=[wsb_b])
        return wsb, wsb_b

    def emit_l0_attn(self, xn16_name, out_name):
        p, nc = self.p, self.nc
        xn_h, xn_b = self.d(xn16_name)
        out_h, out_b = self.d(out_name)
        q_h, q_b = self.d("l0_q")
        vin_h, vin_b = self.d("l0_vkin")
        vout_h, vout_b = self.d("l0_vkout")

        ones_col, ones_col_b = p.tile("ones_col", [128, 1], F32)
        ones16, ones16_b = p.tile("ones_col16", [128, 1], BF16)
        ones_row, ones_row_b = p.tile("ones_row", [1, 128], F32)
        ident, ident_b = p.tile("ident_sb", [128, 128], F32)

        wq, wq_b = self.load_weight_bf16("l0_wqkv_t", 3072, "wA")

        vkps = [self.pb(0), self.pb(1)]
        ksps = [self.pb(2), self.pb(3)]

        vk_sb, vk_sb_b = p.tile("l0_vk", [128, 1024], F32)
        ks_sb, ks_sb_b = p.tile("l0_ks", [1, 1024], F32)
        xt = [p.tile(f"l0_x{c}", [128, BW], BF16) for c in range(NC8)]
        ktm = [p.tile(f"l0_k{sub}", [128, 1024], BF16) for sub in range(4)]
        vtm = [p.tile(f"l0_v{sub}", [128, 1024], BF16) for sub in range(4)]
        qsb = [p.tile(f"l0_qs{i}", [128, BW], BF16) for i in range(2)]

        nmm_vk = 0
        for nb in range(NBLK):
            for c in range(NC8):
                t, b = xt[c]
                self.load(t[:, :], xn_h[128 * c:128 * (c + 1), nb * BW:(nb + 1) * BW], b, rbuf=xn_b)
            for ob in range(8):
                ps, ps_b = self.pb(4 + ob % 2)
                for c in range(NC8):
                    p.emit("tensor", lambda e, c=c, ob=ob, ps=ps: e.matmul(
                        ps[:, :], wq[:, c * 3072 + 128 * ob:c * 3072 + 128 * (ob + 1)],
                        xt[c][0][:, :], start=(c == 0), stop=(c == NC8 - 1),
                        skip_group_check=True),
                        reads=[xt[c][1], wq_b], writes=[ps_b])
                qt, qt_b = qsb[ob % 2]
                p.emit("scalar", lambda e, ps=ps, qt=qt: e.activation(qt[:, :], ps[:, :], AF.Relu),
                       reads=[ps_b], writes=[qt_b])
                self.store(q_h[128 * ob:128 * (ob + 1), nb * BW:(nb + 1) * BW], qt[:, :], qt_b, wbuf=q_b)
            for sub in range(4):
                for half in range(2):
                    for (dst, base) in ((ktm, 1024), (vtm, 2048)):
                        ps, ps_b = self.pb(4 + half)
                        col0 = base + half * 512
                        for c in range(NC8):
                            p.emit("tensor", lambda e, c=c, sub=sub, ps=ps, col0=col0: e.matmul(
                                ps[:, :], xt[c][0][:, sub * 128:(sub + 1) * 128],
                                wq[:, c * 3072 + col0:c * 3072 + col0 + 512],
                                start=(c == 0), stop=(c == NC8 - 1), skip_group_check=True),
                                reads=[xt[c][1], wq_b], writes=[ps_b])
                        t, b = dst[sub]
                        fn = AF.Relu if dst is ktm else AF.Identity
                        p.emit("scalar", lambda e, ps=ps, t=t, half=half, fn=fn: e.activation(
                            t[:, half * 512:half * 512 + 512], ps[:, :], fn),
                            reads=[ps_b], writes=[b])
            if "l0_kvdbg" in self.dump:
                kv_h, kv_b = self.d("l0_kvdbg")
                for sub in range(4):
                    self.store(kv_h[0, nb * 512 + sub * 128:nb * 512 + (sub + 1) * 128, :],
                               ktm[sub][0][:, :], ktm[sub][1], wbuf=kv_b)
                    self.store(kv_h[1, nb * 512 + sub * 128:nb * 512 + (sub + 1) * 128, :],
                               vtm[sub][0][:, :], vtm[sub][1], wbuf=kv_b)
            for g in range(8):
                ps, ps_b = vkps[g % 2]
                for sub in range(4):
                    p.emit("tensor", lambda e, sub=sub, g=g, ps=ps, nb=nb: e.matmul(
                        ps[:, 0:128],
                        vtm[sub][0][:, 128 * g:128 * (g + 1)],
                        ktm[sub][0][:, 128 * g:128 * (g + 1)],
                        start=(sub == 0), stop=(sub == 3),
                        skip_group_check=True),
                        reads=[vtm[sub][1], ktm[sub][1]], writes=[ps_b])
                if nb == 0:
                    p.emit("vector", lambda e, g=g, ps=ps: e.tensor_copy(
                        vk_sb[:, 128 * g:128 * (g + 1)], ps[:, 0:128]),
                        reads=[ps_b], writes=[vk_sb_b])
                else:
                    p.emit("vector", lambda e, g=g, ps=ps: e.tensor_tensor(
                        vk_sb[:, 128 * g:128 * (g + 1)],
                        vk_sb[:, 128 * g:128 * (g + 1)], ps[:, 0:128], ALU.add),
                        reads=[ps_b], writes=[vk_sb_b])
            for sub in range(4):
                for half in range(2):
                    ps, ps_b = ksps[half]
                    p.emit("tensor", lambda e, sub=sub, half=half, ps=ps, nb=nb: e.matmul(
                        ps[0:1, :], ones16[:, :], ktm[sub][0][:, half * 512:half * 512 + 512],
                        start=(sub == 0), stop=(sub == 3),
                        skip_group_check=True),
                        reads=[ktm[sub][1], ones16_b], writes=[ps_b])
            for half in range(2):
                ps2, ps2_b = ksps[half]
                if nb == 0:
                    p.emit("vector", lambda e, ps2=ps2, half=half: e.tensor_copy(
                        ks_sb[0:1, 512 * half:512 * (half + 1)], ps2[0:1, :]),
                        reads=[ps2_b], writes=[ks_sb_b])
                else:
                    p.emit("vector", lambda e, ps2=ps2, half=half: e.tensor_tensor(
                        ks_sb[0:1, 512 * half:512 * (half + 1)],
                        ks_sb[0:1, 512 * half:512 * (half + 1)], ps2[0:1, :], ALU.add),
                        reads=[ps2_b], writes=[ks_sb_b])

        self.store(vin_h[0:128, :], vk_sb[:, :], vk_sb_b, wbuf=vin_b)
        self.store(vin_h[128:129, :], ks_sb[:, :], ks_sb_b, wbuf=vin_b)
        p.emit("gpsimd", lambda e: e.collective_compute(
            "AllReduce", ALU.add, PAIRS, [vin_h.ap().opt()], [vout_h.ap().opt()]),
            reads=[vin_b], writes=[vout_b], kind="coll")
        self.load(vk_sb[:, :], vout_h[0:128, :], vk_sb_b, rbuf=vout_b)
        self.load(ks_sb[:, :], vout_h[128:129, :], ks_sb_b, rbuf=vout_b)

        vkc, vkc_b = p.tile("l0_vkc", [128, 8 * 132], BF16)
        kpad, kpad_b = p.tile("l0_kpad", [128, 128], F32)
        for g in range(8):
            ps, ps_b = self.pb(4 + g % 2)
            p.emit("tensor", lambda e, g=g, ps=ps: e.transpose(
                ps[:, 0:128], vk_sb[:, 128 * g:128 * (g + 1)], ident[:, :]),
                reads=[vk_sb_b, ident_b], writes=[ps_b])
            for j in range(4):
                p.emit("vector", lambda e, g=g, j=j, ps=ps: e.tensor_copy(
                    vkc[:, 132 * g + 33 * j:132 * g + 33 * j + 32], ps[:, 32 * j:32 * (j + 1)]),
                    reads=[ps_b], writes=[vkc_b])
        for g in range(8):
            p.emit("vector", lambda e, g=g: e.tensor_copy(
                kpad[0:1, :], ks_sb[0:1, 128 * g:128 * (g + 1)]),
                reads=[ks_sb_b], writes=[kpad_b])
            ps, ps_b = self.pb(6)
            p.emit("tensor", lambda e, ps=ps: e.transpose(ps[:, 0:128], kpad[:, :], ident[:, :]),
                   reads=[kpad_b, ident_b], writes=[ps_b])
            for j in range(4):
                p.emit("vector", lambda e, g=g, j=j, ps=ps: e.tensor_copy(
                    vkc[32 * j:32 * (j + 1), 132 * g + 33 * j + 32:132 * g + 33 * j + 33],
                    ps[32 * j:32 * (j + 1), 0:1]),
                    reads=[ps_b], writes=[vkc_b])

        wp, wp_b = self.load_weight_bf16("l0_wproj_t", 1024, "wB")
        bp_h, bp_db = self.d("l0_bproj")
        bp, bp_b = p.tile("bias_sb", [128, 8], F32)
        self.load(bp[:, :], bp_h[:, :], bp_b, rbuf=bp_db)

        qfull, qfull_b = p.tile("l0_qf", [128, 8 * BW], BF16)
        ctx, ctx_b = p.tile("l0_ctx", [128, 8 * BW], BF16)
        rec, rec_b = p.tile("l0_rec", [1, BW], F32)
        ctmp, ctmp_b = p.tile("l0_ctmp", [128, BW], F32)
        sout, sout_b = p.tile("l0_sout", [128, BW], F32)
        for nb in range(NBLK):
            for c in range(NC8):
                self.load(qfull[:, c * BW:(c + 1) * BW],
                          q_h[128 * c:128 * (c + 1), nb * BW:(nb + 1) * BW], qfull_b, rbuf=q_b)
            for h in range(32):
                g, j = h // 4, h % 4
                ps, ps_b = self.pb(4 + h % 2)
                p.emit("tensor", lambda e, g=g, j=j, ps=ps: e.matmul(
                    ps[0:33, :], vkc[32 * j:32 * (j + 1), 132 * g + 33 * j:132 * g + 33 * (j + 1)],
                    qfull[32 * j:32 * (j + 1), g * BW:(g + 1) * BW],
                    start=True, stop=True, skip_group_check=True,
                    tile_position=(32 * j, 0)),
                    reads=[vkc_b, qfull_b], writes=[ps_b])
                p.emit("vector", lambda e, ps=ps: e.tensor_scalar(
                    rec[0:1, :], ps[32:33, :], self.cap(1e-15, 1), None, ALU.add),
                    reads=[ps_b, self.cvec_buf()], writes=[rec_b])
                p.emit("vector", lambda e: e.reciprocal(rec[0:1, :], rec[0:1, :]),
                       reads=[], writes=[rec_b])
                ps2, ps2_b = self.pb(6)
                p.emit("tensor", lambda e, ps2=ps2: e.matmul(
                    ps2[0:32, :], ones_row[0:1, 0:32], rec[0:1, :],
                    start=True, stop=True, skip_group_check=True),
                    reads=[rec_b, ones_row_b], writes=[ps2_b])
                p.emit("vector", lambda e, ps=ps: e.tensor_copy(ctmp[0:32, :], ps[0:32, :]),
                       reads=[ps_b], writes=[ctmp_b])
                p.emit("vector", lambda e, g=g, j=j, ps2=ps2: e.tensor_tensor(
                    ctx[32 * j:32 * (j + 1), g * BW:(g + 1) * BW], ctmp[0:32, :],
                    ps2[0:32, :], ALU.mult),
                    reads=[ctmp_b, ps2_b], writes=[ctx_b])
            for ob in range(8):
                ps, ps_b = self.pb(4 + ob % 2)
                for c in range(NC8):
                    p.emit("tensor", lambda e, c=c, ob=ob, ps=ps: e.matmul(
                        ps[:, :], wp[:, c * 1024 + 128 * ob:c * 1024 + 128 * (ob + 1)],
                        ctx[:, c * BW:(c + 1) * BW], start=(c == 0), stop=(c == NC8 - 1),
                        skip_group_check=True),
                        reads=[ctx_b, wp_b], writes=[ps_b])
                p.emit("scalar", lambda e, ps=ps, ob=ob: e.activation(
                    sout[:, :], ps[:, :], AF.Identity, bias=bp[:, ob:ob + 1]),
                    reads=[ps_b, bp_b], writes=[sout_b])
                self.store(out_h[128 * ob:128 * (ob + 1), nb * BW:(nb + 1) * BW],
                           sout[:, :], sout_b, wbuf=out_b)

    def emit_fp4_quant(self, X, W):
        p = self.p
        Xt, Xb = X
        A = p.tile("q_a", [128, 1024], F32)
        Bt = p.tile("q_b", [128, 1024], F32)
        R1 = p.tile("q_r1", [128, 1024], F32)
        R2 = p.tile("q_r2", [128, 1024], F32)
        Cs = p.tile("q_c", [128, 1024], F32)
        SM = p.tile("q_sm", [128, 128], F32)
        cvb = self.cvec_buf()

        def sact(dst, src, fn, bias=None, n=None):
            d, db = dst
            s, sb = src
            kw = {}
            rd = [sb]
            if bias is not None:
                kw["bias"] = self.cap(bias)
                rd.append(cvb)
            p.emit("scalar", lambda e: e.activation(d[:, 0:W], s[:, 0:W], fn, **kw),
                   reads=rd, writes=[db])

        sact(A, X, AF.Abs)
        p.emit("vector", lambda e: e.tensor_tensor(Bt[0][:, 0:W // 2], A[0][:, 0:W:2], A[0][:, 1:W:2], ALU.max),
               reads=[A[1]], writes=[Bt[1]])
        p.emit("vector", lambda e: e.tensor_tensor(A[0][:, 0:W // 4], Bt[0][:, 0:W // 2:2], Bt[0][:, 1:W // 2:2], ALU.max),
               reads=[Bt[1]], writes=[A[1]])
        p.emit("vector", lambda e: e.tensor_tensor(Bt[0][:, 0:W // 8], A[0][:, 0:W // 4:2], A[0][:, 1:W // 4:2], ALU.max),
               reads=[A[1]], writes=[Bt[1]])
        p.emit("vector", lambda e: e.tensor_tensor(SM[0][:, 0:W // 16], Bt[0][:, 0:W // 8:2], Bt[0][:, 1:W // 8:2], ALU.max),
               reads=[Bt[1]], writes=[SM[1]])
        p.emit("vector", lambda e: e.tensor_scalar(SM[0][:, 0:W // 16], SM[0][:, 0:W // 16], 1.0 / 6.0, 1e-12, ALU.mult, ALU.max),
               reads=[], writes=[SM[1]])
        p.emit("vector", lambda e: e.reciprocal(SM[0][:, 64:64 + W // 16], SM[0][:, 0:W // 16]),
               reads=[], writes=[SM[1]])
        for (dst, off) in ((Cs, 0), (Bt, 64)):
            p.emit("vector", lambda e, dst=dst, off=off: e.tensor_copy(dst[0][:, 0:W:16], SM[0][:, off:off + W // 16]),
                   reads=[SM[1]], writes=[dst[1]])
            for stp in (16, 8, 4, 2):
                p.emit("vector", lambda e, dst=dst, stp=stp: e.tensor_copy(
                    dst[0][:, stp // 2:W:stp], dst[0][:, 0:W:stp]),
                    reads=[], writes=[dst[1]])
        p.emit("vector", lambda e: e.tensor_tensor(Bt[0][:, 0:W], Xt[:, 0:W], Bt[0][:, 0:W], ALU.mult),
               reads=[Xb], writes=[Bt[1]])
        sact(A, Bt, AF.Abs)
        sact(R1, A, AF.Relu, bias=-2.0)
        sact(R2, A, AF.Relu, bias=-4.0)
        p.emit("vector", lambda e: e.scalar_tensor_tensor(R1[0][:, 0:W], R2[0][:, 0:W], 0.5, R1[0][:, 0:W], ALU.mult, ALU.add),
               reads=[R2[1]], writes=[R1[1]])
        p.emit("vector", lambda e: e.scalar_tensor_tensor(A[0][:, 0:W], A[0][:, 0:W], 2.0, R1[0][:, 0:W], ALU.mult, ALU.subtract),
               reads=[R1[1]], writes=[A[1]])
        p.emit("vector", lambda e: e.tensor_scalar(A[0][:, 0:W], A[0][:, 0:W], MAGIC, None, ALU.add),
               reads=[], writes=[A[1]])
        p.emit("vector", lambda e: e.tensor_scalar(A[0][:, 0:W], A[0][:, 0:W], MAGIC, None, ALU.subtract),
               reads=[], writes=[A[1]])
        sact(R1, A, AF.Relu, bias=-4.0)
        sact(R2, A, AF.Relu, bias=-6.0)
        p.emit("vector", lambda e: e.scalar_tensor_tensor(R1[0][:, 0:W], R1[0][:, 0:W], 0.5, R2[0][:, 0:W], ALU.mult, ALU.add),
               reads=[R2[1]], writes=[R1[1]])
        p.emit("vector", lambda e: e.scalar_tensor_tensor(A[0][:, 0:W], A[0][:, 0:W], 0.5, R1[0][:, 0:W], ALU.mult, ALU.add),
               reads=[R1[1]], writes=[A[1]])
        p.emit("scalar", lambda e: e.activation(Bt[0][:, 0:W], Bt[0][:, 0:W], AF.Sign),
               reads=[], writes=[Bt[1]])
        p.emit("vector", lambda e: e.tensor_tensor(A[0][:, 0:W], A[0][:, 0:W], Bt[0][:, 0:W], ALU.mult),
               reads=[Bt[1]], writes=[A[1]])
        p.emit("vector", lambda e: e.tensor_tensor(Bt[0][:, 0:W], A[0][:, 0:W], Cs[0][:, 0:W], ALU.mult),
               reads=[A[1], Cs[1]], writes=[Bt[1]])
        return Bt

    def emit_weight_quant(self, src_name, dst_name, R, C):
        p = self.p
        src_h, src_b = self.d(src_name)
        dst_h, dst_b = self.d(dst_name)
        ident, ident_b = p.tile("ident_sb", [128, 128], F32)
        stg = p.tile("q_x", [128, 1024], F32)
        ev = p.tile("q_ev", [128, 128], BF16)
        for r in range(R // 128):
            for cc in range(ceil_div(C, 1024)):
                wcols = min(1024, C - cc * 1024)
                self.load(stg[0][:, 0:wcols], src_h[128 * r:128 * (r + 1), cc * 1024:cc * 1024 + wcols],
                          stg[1], rbuf=src_b)
                qt = self.emit_fp4_quant(stg, wcols)
                for c in range(wcols // 128):
                    ps, ps_b = self.pb(6)
                    p.emit("tensor", lambda e, c=c, ps=ps, qt=qt: e.transpose(
                        ps[:, 0:128], qt[0][:, 128 * c:128 * (c + 1)], ident[:, :]),
                        reads=[qt[1], ident_b], writes=[ps_b])
                    p.emit("vector", lambda e, ps=ps: e.tensor_copy(ev[0][:, :], ps[:, 0:128]),
                           reads=[ps_b], writes=[ev[1]])
                    self.store(dst_h[cc * 1024 + 128 * c:cc * 1024 + 128 * (c + 1), 128 * r:128 * (r + 1)],
                               ev[0][:, :], ev[1], wbuf=dst_b)

    def emit_l0_ffn_and_gate(self, xn32_name, x_name, s_name, x1_name):
        p = self.p
        xn_h, xn_b = self.d(xn32_name)
        x_h, x_b = self.d(x_name)
        s_h, s_b = self.d(s_name)
        x1_h, x1_b = self.d(x1_name)
        wg_h, wg_b = self.d("l0_wgq_t")
        wu_h, wu_b = self.d("l0_wuq_t")
        wd_h, wd_b = self.d("l0_wdq_t")
        ident, ident_b = p.tile("ident_sb", [128, 128], F32)
        cvb = self.cvec_buf()
        HID = 2816

        self.emit_weight_quant("l0_wg", "l0_wgq_t", HID, 1024)
        self.emit_weight_quant("l0_wu", "l0_wuq_t", HID, 1024)
        self.emit_weight_quant("l0_wd", "l0_wdq_t", 1024, HID)

        xtm = p.tile("f_xtm", [128, 1024], F32)
        xqf = p.tile("f_xqf", [128, 8 * 128], BF16)
        wgt = p.tile("f_wg", [128, 8 * 512], BF16)
        wut = p.tile("f_wu", [128, 8 * 512], BF16)
        wdt = p.tile("f_wd", [128, 22 * 512], BF16)
        hq = p.tile("f_hq", [128, HID], F32)
        hqf = p.tile("f_hqf", [128, 22 * 128], BF16)
        u1 = p.tile("f_u1", [128, 512], F32)
        e1 = p.tile("f_e1", [128, 512], F32)
        h1 = p.tile("f_h1", [128, 512], F32)
        gtm = p.tile("f_gtm", [128, 1024], F32)
        gfm = p.tile("f_gfm", [128, 128], F32)
        xg = p.tile("f_xg", [128, 128], F32)
        sg = p.tile("f_sg", [128, 128], F32)

        for tt in range(T // 128):
            for c in range(NC8):
                ps, ps_b = self.pb(5)
                self.load(xg[0][:, :], xn_h[128 * c:128 * (c + 1), 128 * tt:128 * (tt + 1)], xg[1], rbuf=xn_b)
                p.emit("tensor", lambda e, ps=ps: e.transpose(ps[:, 0:128], xg[0][:, :], ident[:, :]),
                       reads=[xg[1], ident_b], writes=[ps_b])
                p.emit("vector", lambda e, c=c, ps=ps: e.tensor_copy(xtm[0][:, 128 * c:128 * (c + 1)], ps[:, 0:128]),
                       reads=[ps_b], writes=[xtm[1]])
            xq = self.emit_fp4_quant(xtm, 1024)
            for c in range(NC8):
                ps, ps_b = self.pb(5)
                p.emit("tensor", lambda e, c=c, ps=ps, xq=xq: e.transpose(
                    ps[:, 0:128], xq[0][:, 128 * c:128 * (c + 1)], ident[:, :]),
                    reads=[xq[1], ident_b], writes=[ps_b])
                p.emit("scalar", lambda e, c=c, ps=ps: e.activation(
                    xqf[0][:, 128 * c:128 * (c + 1)], ps[:, 0:128], AF.Identity),
                    reads=[ps_b], writes=[xqf[1]])
            for ob in range(6):
                oc = min(512, HID - 512 * ob)
                for c in range(NC8):
                    self.load(wgt[0][:, c * 512:c * 512 + oc],
                              wg_h[128 * c:128 * (c + 1), 512 * ob:512 * ob + oc], wgt[1], rbuf=wg_b)
                    self.load(wut[0][:, c * 512:c * 512 + oc],
                              wu_h[128 * c:128 * (c + 1), 512 * ob:512 * ob + oc], wut[1], rbuf=wu_b)
                psg, psg_b = self.pb(0)
                psu, psu_b = self.pb(1)
                for c in range(NC8):
                    p.emit("tensor", lambda e, c=c, oc=oc: e.matmul(
                        psg[:, 0:oc], xqf[0][:, 128 * c:128 * (c + 1)], wgt[0][:, c * 512:c * 512 + oc],
                        start=(c == 0), stop=(c == NC8 - 1), skip_group_check=True),
                        reads=[xqf[1], wgt[1]], writes=[psg_b])
                for c in range(NC8):
                    p.emit("tensor", lambda e, c=c, oc=oc: e.matmul(
                        psu[:, 0:oc], xqf[0][:, 128 * c:128 * (c + 1)], wut[0][:, c * 512:c * 512 + oc],
                        start=(c == 0), stop=(c == NC8 - 1), skip_group_check=True),
                        reads=[xqf[1], wut[1]], writes=[psu_b])
                p.emit("vector", lambda e, oc=oc: e.tensor_copy(u1[0][:, 0:oc], psu[:, 0:oc]),
                       reads=[psu_b], writes=[u1[1]])
                p.emit("scalar", lambda e, oc=oc: e.activation(e1[0][:, 0:oc], psg[:, 0:oc], AF.Exp,
                                                               scale=self.cap(-1.0)),
                       reads=[psg_b, cvb], writes=[e1[1]])
                p.emit("vector", lambda e, oc=oc: e.tensor_scalar(e1[0][:, 0:oc], e1[0][:, 0:oc], 1.0, None, ALU.add),
                       reads=[], writes=[e1[1]])
                p.emit("vector", lambda e, oc=oc: e.reciprocal(e1[0][:, 0:oc], e1[0][:, 0:oc]),
                       reads=[], writes=[e1[1]])
                p.emit("vector", lambda e, oc=oc: e.tensor_tensor(h1[0][:, 0:oc], psg[:, 0:oc], e1[0][:, 0:oc], ALU.mult),
                       reads=[psg_b, e1[1]], writes=[h1[1]])
                p.emit("vector", lambda e, oc=oc: e.tensor_tensor(h1[0][:, 0:oc], h1[0][:, 0:oc], u1[0][:, 0:oc], ALU.mult),
                       reads=[u1[1]], writes=[h1[1]])
                hqt = self.emit_fp4_quant(h1, oc)
                p.emit("vector", lambda e, ob=ob, oc=oc, hqt=hqt: e.tensor_copy(
                    hq[0][:, 512 * ob:512 * ob + oc], hqt[0][:, 0:oc]),
                    reads=[hqt[1]], writes=[hq[1]])
            for c in range(22):
                ps, ps_b = self.pb(5)
                p.emit("tensor", lambda e, c=c, ps=ps: e.transpose(
                    ps[:, 0:128], hq[0][:, 128 * c:128 * (c + 1)], ident[:, :]),
                    reads=[hq[1], ident_b], writes=[ps_b])
                p.emit("scalar", lambda e, c=c, ps=ps: e.activation(
                    hqf[0][:, 128 * c:128 * (c + 1)], ps[:, 0:128], AF.Identity),
                    reads=[ps_b], writes=[hqf[1]])
            for o2 in range(2):
                for c in range(22):
                    self.load(wdt[0][:, c * 512:(c + 1) * 512],
                              wd_h[128 * c:128 * (c + 1), 512 * o2:512 * (o2 + 1)], wdt[1], rbuf=wd_b)
                psd, psd_b = self.pb(2)
                for c in range(22):
                    p.emit("tensor", lambda e, c=c: e.matmul(
                        psd[:, :], hqf[0][:, 128 * c:128 * (c + 1)], wdt[0][:, c * 512:(c + 1) * 512],
                        start=(c == 0), stop=(c == 21), skip_group_check=True),
                        reads=[hqf[1], wdt[1]], writes=[psd_b])
                p.emit("scalar", lambda e: e.activation(e1[0][:, :], psd[:, :], AF.Exp, scale=self.cap(-1.0)),
                       reads=[psd_b, cvb], writes=[e1[1]])
                p.emit("vector", lambda e: e.tensor_scalar(e1[0][:, :], e1[0][:, :], 1.0, None, ALU.add),
                       reads=[], writes=[e1[1]])
                p.emit("vector", lambda e: e.reciprocal(e1[0][:, :], e1[0][:, :]),
                       reads=[], writes=[e1[1]])
                p.emit("vector", lambda e, o2=o2: e.tensor_tensor(
                    gtm[0][:, 512 * o2:512 * (o2 + 1)], psd[:, :], e1[0][:, :], ALU.mult),
                    reads=[psd_b, e1[1]], writes=[gtm[1]])
            for c in range(NC8):
                ps, ps_b = self.pb(5)
                p.emit("tensor", lambda e, c=c, ps=ps: e.transpose(
                    ps[:, 0:128], gtm[0][:, 128 * c:128 * (c + 1)], ident[:, :]),
                    reads=[gtm[1], ident_b], writes=[ps_b])
                self.load(xg[0][:, :], x_h[128 * c:128 * (c + 1), 128 * tt:128 * (tt + 1)], xg[1], rbuf=x_b)
                self.load(sg[0][:, :], s_h[128 * c:128 * (c + 1), 128 * tt:128 * (tt + 1)], sg[1], rbuf=s_b)
                p.emit("vector", lambda e, ps=ps: e.tensor_tensor(gfm[0][:, :], sg[0][:, :], ps[:, 0:128], ALU.mult),
                       reads=[sg[1], ps_b], writes=[gfm[1]])
                p.emit("vector", lambda e: e.tensor_tensor(gfm[0][:, :], gfm[0][:, :], xg[0][:, :], ALU.add),
                       reads=[xg[1]], writes=[gfm[1]])
                self.store(x1_h[128 * c:128 * (c + 1), 128 * tt:128 * (tt + 1)], gfm[0][:, :], gfm[1], wbuf=x1_b)

    def emit_l1_attn(self, xn16_name, out_name):
        p, nc = self.p, self.nc
        xn_h, xn_b = self.d(xn16_name)
        out_h, out_b = self.d(out_name)
        xf_h, xf_b = self.d("l1_xfull")
        q_h, q_b = self.d("l1_q")
        k_h, k_b = self.d("l1_k")
        v_h, v_b = self.d("l1_v")
        ekb_h, ekb_b = self.d("l1_ekb")
        ones_row, ones_row_b = p.tile("ones_row", [1, 128], F32)
        cvb = self.cvec_buf()
        EKW = 7680

        p.emit("gpsimd", lambda e: e.collective_compute(
            "AllGather", ALU.bypass, PAIRS, [xn_h.ap().opt()], [xf_h.ap().opt()]),
            reads=[xn_b], writes=[xf_b], kind="coll")

        pa_h, pa_db = self.d("l1_pa")
        pa, pa_b = p.tile("l1_pa_sb", [1, 32], F32)
        self.load(pa[:, :], pa_h[:, :], pa_b, rbuf=pa_db)
        p.emit("vector", lambda e: e.tensor_scalar(pa[0:1, :], pa[0:1, :], 0.01, None, ALU.max),
               reads=[], writes=[pa_b])

        dv_h, dv_db = self.d("l1_dvec")
        CW = 1920
        dv = p.tile("l1_dv", [1, CW], F32)
        row = p.tile("l1_row", [1, CW], F32)
        row16 = p.tile("l1_row16", [1, CW], BF16)
        for ck in range(EKW // CW):
            self.load(dv[0][0:1, :], dv_h[0:1, ck * CW:(ck + 1) * CW], dv[1], rbuf=dv_db)
            for h in range(16):
                p.emit("scalar", lambda e, h=h: e.activation(
                    row[0][0:1, :], dv[0][0:1, :], AF.Ln, scale=pa[0:1, 16 + h:17 + h], bias=self.cap(1.0, 1)),
                    reads=[dv[1], pa_b, cvb], writes=[row[1]])
                p.emit("vector", lambda e, h=h: e.tensor_scalar(
                    row[0][0:1, :], row[0][0:1, :], pa[0:1, h:h + 1], -1.0, ALU.mult, ALU.mult),
                    reads=[pa_b], writes=[row[1]])
                p.emit("scalar", lambda e: e.activation(row16[0][0:1, :], row[0][0:1, :], AF.Exp),
                       reads=[row[1]], writes=[row16[1]])
                self.store(ekb_h[h:h + 1, ck * CW:(ck + 1) * CW], row16[0][0:1, :], row16[1], wbuf=ekb_b)

        fr_h, fr_db = self.d("l1_freqs_rep")
        po_h, po_db = self.d("l1_pos_own")
        pl_h, pl_db = self.d("l1_pos_all")
        fr, fr_b = p.tile("l1_fr", [1, 128], F32)
        pos = p.tile("l1_pos", [1, BW], F32)
        self.load(fr[:, :], fr_h[:, :], fr_b, rbuf=fr_db)
        ct_o = p.tile("l1_ct_o", [128, T], BF16)
        st_o = p.tile("l1_st_o", [128, T], BF16)
        ct_a = p.tile("l1_ct_a", [128, 4096], BF16)
        st_a = p.tile("l1_st_a", [128, 4096], BF16)
        wrk = p.tile("l1_wrk", [128, BW], F32)
        wk2 = p.tile("l1_wk2", [128, BW], F32)
        INV2PI = float(1.0 / (2 * np.pi))
        TWO_PI = 2 * np.pi
        C1 = 6.283203125
        C2 = float(np.float32(TWO_PI - C1))
        C3 = float(np.float32(TWO_PI - C1 - np.float32(C2)))
        PIF = float(np.float32(np.pi))

        def rope_tables(psrc_h, psrc_db, width, ct, st):
            for nb in range(width // BW):
                self.load(pos[0][0:1, :], psrc_h[0:1, nb * BW:(nb + 1) * BW], pos[1], rbuf=psrc_db)
                ps, ps_b = self.pb(6)
                p.emit("tensor", lambda e, ps=ps: e.matmul(
                    ps[:, :], fr[0:1, :], pos[0][0:1, :],
                    start=True, stop=True, skip_group_check=True),
                    reads=[fr_b, pos[1]], writes=[ps_b])
                p.emit("vector", lambda e, ps=ps: e.tensor_scalar(
                    wrk[0][:, :], ps[:, :], INV2PI, MAGIC, ALU.mult, ALU.add),
                    reads=[ps_b], writes=[wrk[1]])
                p.emit("vector", lambda e: e.tensor_scalar(
                    wrk[0][:, :], wrk[0][:, :], MAGIC, None, ALU.subtract),
                    reads=[], writes=[wrk[1]])
                p.emit("vector", lambda e, ps=ps: e.scalar_tensor_tensor(
                    wk2[0][:, :], wrk[0][:, :], -C1, ps[:, :], ALU.mult, ALU.add),
                    reads=[wrk[1], ps_b], writes=[wk2[1]])
                p.emit("vector", lambda e: e.scalar_tensor_tensor(
                    wk2[0][:, :], wrk[0][:, :], -C2, wk2[0][:, :], ALU.mult, ALU.add),
                    reads=[wrk[1]], writes=[wk2[1]])
                p.emit("vector", lambda e: e.scalar_tensor_tensor(
                    wk2[0][:, :], wrk[0][:, :], -C3, wk2[0][:, :], ALU.mult, ALU.add),
                    reads=[wrk[1]], writes=[wk2[1]])
                p.emit("vector", lambda e: e.tensor_scalar(
                    wk2[0][:, :], wk2[0][:, :], PIF, -PIF, ALU.min, ALU.max),
                    reads=[], writes=[wk2[1]])
                p.emit("scalar", lambda e, nb=nb, st=st: e.activation(
                    st[0][:, nb * BW:(nb + 1) * BW], wk2[0][:, :], AF.Sin),
                    reads=[wk2[1]], writes=[st[1]])
                p.emit("scalar", lambda e: e.activation(wrk[0][:, :], wk2[0][:, :], AF.Abs),
                       reads=[wk2[1]], writes=[wrk[1]])
                p.emit("scalar", lambda e, nb=nb, ct=ct: e.activation(
                    ct[0][:, nb * BW:(nb + 1) * BW], wrk[0][:, :], AF.Sin,
                    scale=self.cap(-1.0), bias=self.cap(np.pi / 2)),
                    reads=[wrk[1], cvb], writes=[ct[1]])

        rope_tables(po_h, po_db, T, ct_o, st_o)
        rope_tables(pl_h, pl_db, 4096, ct_a, st_a)

        wq, wq_b = self.load_weight_bf16("l1_wqkv_t", 3072, "wA")

        qt = [p.tile(f"l1_qt{i}", [128, BW], BF16) for i in range(8)]
        rp1 = p.tile("l1_rp1", [128, BW], BF16)
        rp2 = p.tile("l1_rp2", [128, BW], BF16)
        rr1 = p.tile("l1_rr1", [128, BW], BF16)

        def qkv_block(rhs_list, col0, ct, st, dst_h, dst_b, tab_off):
            for ch in range(8):
                ps, ps_b = self.pb(4 + ch % 2)
                for c in range(NC8):
                    rhs, rhs_b = rhs_list[c]
                    p.emit("tensor", lambda e, c=c, ch=ch, ps=ps, rhs=rhs, rhs_b=rhs_b: e.matmul(
                        ps[:, :], wq[:, c * 3072 + col0 + 128 * ch:c * 3072 + col0 + 128 * (ch + 1)],
                        rhs, start=(c == 0), stop=(c == NC8 - 1), skip_group_check=True),
                        reads=[rhs_b, wq_b], writes=[ps_b])
                t, tb = qt[ch]
                p.emit("scalar", lambda e, ps=ps, t=t: e.activation(t[:, :], ps[:, :], AF.Identity),
                       reads=[ps_b], writes=[tb])
            for cp in range(4):
                q1, q1b = qt[cp]
                q2, q2b = qt[4 + cp]
                cs = ct[0][:, tab_off:tab_off + BW]
                sn = st[0][:, tab_off:tab_off + BW]
                p.emit("vector", lambda e, q1=q1, cs=cs: e.tensor_tensor(rp1[0][:, :], q1[:, :], cs, ALU.mult),
                       reads=[q1b, ct[1]], writes=[rp1[1]])
                p.emit("vector", lambda e, q2=q2, sn=sn: e.tensor_tensor(rp2[0][:, :], q2[:, :], sn, ALU.mult),
                       reads=[q2b, st[1]], writes=[rp2[1]])
                p.emit("vector", lambda e: e.tensor_tensor(rr1[0][:, :], rp1[0][:, :], rp2[0][:, :], ALU.subtract),
                       reads=[rp1[1], rp2[1]], writes=[rr1[1]])
                self.store(dst_h[128 * cp:128 * (cp + 1), tab_off:tab_off + BW], rr1[0][:, :], rr1[1], wbuf=dst_b)
                p.emit("vector", lambda e, q1=q1, sn=sn: e.tensor_tensor(rp1[0][:, :], q1[:, :], sn, ALU.mult),
                       reads=[q1b, st[1]], writes=[rp1[1]])
                p.emit("vector", lambda e, q2=q2, cs=cs: e.tensor_tensor(rp2[0][:, :], q2[:, :], cs, ALU.mult),
                       reads=[q2b, ct[1]], writes=[rp2[1]])
                p.emit("vector", lambda e: e.tensor_tensor(rr1[0][:, :], rp1[0][:, :], rp2[0][:, :], ALU.add),
                       reads=[rp1[1], rp2[1]], writes=[rr1[1]])
                self.store(dst_h[512 + 128 * cp:512 + 128 * (cp + 1), tab_off:tab_off + BW], rr1[0][:, :], rr1[1], wbuf=dst_b)

        xt = [p.tile(f"l1_x{c}", [128, BW], BF16) for c in range(NC8)]
        for nb in range(NBLK):
            for c in range(NC8):
                t, b = xt[c]
                self.load(t[:, :], xn_h[128 * c:128 * (c + 1), nb * BW:(nb + 1) * BW], b, rbuf=xn_b)
            qkv_block([(t[:, :], b) for (t, b) in xt], 0, ct_o, st_o, q_h, q_b, nb * BW)
        xg = [p.tile(f"l1_xg{c}", [128, BW], BF16) for c in range(NC8)]
        vt_sb = p.tile("l1_vt", [128, 1040], BF16)
        for kb in range(8):
            r = kb // 4
            for c in range(NC8):
                t, b = xg[c]
                self.load(t[:, :], xf_h[1024 * r + 128 * c:1024 * r + 128 * (c + 1),
                                        (kb % 4) * BW:(kb % 4 + 1) * BW], b, rbuf=xf_b)
            qkv_block([(t[:, :], b) for (t, b) in xg], 1024, ct_a, st_a, k_h, k_b, kb * BW)
            for sub in range(4):
                p.emit("vector", lambda e: e.memset(vt_sb[0][:, :], 1.0), reads=[], writes=[vt_sb[1]])
                for vb in range(2):
                    ps, ps_b = self.pb(4 + vb)
                    for c in range(NC8):
                        t, b = xg[c]
                        p.emit("tensor", lambda e, c=c, sub=sub, vb=vb, ps=ps, t=t, b=b: e.matmul(
                            ps[:, :], t[:, sub * 128:(sub + 1) * 128],
                            wq[:, c * 3072 + 2048 + 512 * vb:c * 3072 + 2048 + 512 * (vb + 1)],
                            start=(c == 0), stop=(c == NC8 - 1), skip_group_check=True),
                            reads=[b, wq_b], writes=[ps_b])
                    p.emit("vector", lambda e, vb=vb, ps=ps: e.tensor_copy(
                        vt_sb[0][:, 65 * 8 * vb:65 * 8 * (vb + 1)].rearrange("p (h d) -> p h d", d=65)[:, :, 0:64],
                        ps[:, :].rearrange("p (h d) -> p h d", d=64)),
                        reads=[ps_b], writes=[vt_sb[1]])
                self.store(v_h[kb * BW + sub * 128:kb * BW + (sub + 1) * 128, :], vt_sb[0][:, :], vt_sb[1], wbuf=v_b)

        self.p.new_phase()
        ksb1 = p.tile("l1_k1", [128, 4096], BF16)
        ksb2 = p.tile("l1_k2", [128, 4096], BF16)
        vsb = p.tile("l1_vs", [128, 32 * 260], BF16)
        qsb1 = p.tile("l1_q1", [128, T], BF16)
        qsb2 = p.tile("l1_q2", [128, T], BF16)
        strips = [p.tile(f"l1_ek{j}", [128, 4480], BF16) for j in range(4)]
        pt = [p.tile(f"l1_p{j}", [128, BW], BF16) for j in range(4)]
        rec = p.tile("l1_rec", [1, BW], F32)
        ytmp = p.tile("l1_ytmp", [128, BW], F32)
        ctx = p.tile("l1_ctx", [128, 8 * T], BF16)

        def gblock(gb):
            return 2 * (gb % 4) + gb // 4

        def kbase(kc):
            return gblock(kc // 4) * BW + (kc % 4) * 128

        for hg in range(4):
            self.load(ksb1[0][:, :], k_h[128 * hg:128 * (hg + 1), :], ksb1[1], rbuf=k_b)
            self.load(ksb2[0][:, :], k_h[512 + 128 * hg:512 + 128 * (hg + 1), :], ksb2[1], rbuf=k_b)
            for tk in range(32):
                self.load(vsb[0][:, 260 * tk:260 * (tk + 1)],
                          v_h[128 * tk:128 * (tk + 1), 65 * 4 * hg:65 * 4 * (hg + 1)], vsb[1], rbuf=v_b)
            self.load(qsb1[0][:, :], q_h[128 * hg:128 * (hg + 1), :], qsb1[1], rbuf=q_b)
            self.load(qsb2[0][:, :], q_h[512 + 128 * hg:512 + 128 * (hg + 1), :], qsb2[1], rbuf=q_b)
            for i in range(4):
                base0 = 3072 - 1024 * i
                for j in range(4):
                    h = 4 * hg + j
                    self.load(strips[j][0][:, :],
                              AP(ekb_h, h * EKW + base0, [[1, 128], [1, 4480]]),
                              strips[j][1], rbuf=ekb_b)
                for kc in range(32):
                    for j in range(4):
                        ps, ps_b = self.pb(j)
                        for half, ksb, qs in ((0, ksb1, qsb1), (1, ksb2, qsb2)):
                            p.emit("tensor", lambda e, j=j, kc=kc, i=i, ps=ps, ksb=ksb, qs=qs, half=half: e.matmul(
                                ps[:, :], ksb[0][32 * j:32 * (j + 1), 128 * kc:128 * (kc + 1)],
                                qs[0][32 * j:32 * (j + 1), BW * i:BW * (i + 1)],
                                start=(half == 0), stop=(half == 1), skip_group_check=True,
                                tile_position=(32 * j, 0)),
                                reads=[ksb[1], qs[1]], writes=[ps_b])
                        p.emit("scalar", lambda e, j=j, ps=ps: e.activation(
                            pt[j][0][:, :], ps[:, :], AF.Exp, scale=self.cap(0.125)),
                            reads=[ps_b, cvb], writes=[pt[j][1]])
                        off = kbase(kc)
                        p.emit("vector", lambda e, j=j, off=off: e.tensor_tensor(
                            pt[j][0][:, :], pt[j][0][:, :],
                            strips[j][0][:, off:off + BW][:, ::-1], ALU.mult),
                            reads=[strips[j][1]], writes=[pt[j][1]])
                        pvps, pvps_b = self.pb(4 + j)
                        p.emit("tensor", lambda e, j=j, kc=kc, pvps=pvps: e.matmul(
                            pvps[0:65, :], vsb[0][:, 260 * kc + 65 * j:260 * kc + 65 * (j + 1)],
                            pt[j][0][:, :], start=(kc == 0), stop=(kc == 31), skip_group_check=True),
                            reads=[vsb[1], pt[j][1]], writes=[pvps_b])
                for j in range(4):
                    h = 4 * hg + j
                    pvps, pvps_b = self.pb(4 + j)
                    p.emit("vector", lambda e, pvps=pvps: e.reciprocal(rec[0][0:1, :], pvps[64:65, :]),
                           reads=[pvps_b], writes=[rec[1]])
                    ps2, ps2_b = self.pb(j)
                    p.emit("tensor", lambda e, ps2=ps2: e.matmul(
                        ps2[0:64, :], ones_row[0:1, 0:64], rec[0][0:1, :],
                        start=True, stop=True, skip_group_check=True),
                        reads=[rec[1], ones_row_b], writes=[ps2_b])
                    p.emit("vector", lambda e, pvps=pvps: e.tensor_copy(ytmp[0][0:64, :], pvps[0:64, :]),
                           reads=[pvps_b], writes=[ytmp[1]])
                    p.emit("vector", lambda e, h=h, i=i, ps2=ps2: e.tensor_tensor(
                        ctx[0][64 * (h % 2):64 * (h % 2) + 64, (h // 2) * T + BW * i:(h // 2) * T + BW * (i + 1)],
                        ytmp[0][0:64, :], ps2[0:64, :], ALU.mult),
                        reads=[ytmp[1], ps2_b], writes=[ctx[1]])

        wo, wo_b = self.load_weight_bf16("l1_wout_t", 1024, "wB")
        sout = p.tile("l1_sout", [128, BW], F32)
        for i in range(4):
            for ob in range(8):
                ps, ps_b = self.pb(4 + ob % 2)
                for c in range(NC8):
                    p.emit("tensor", lambda e, c=c, ob=ob, i=i, ps=ps: e.matmul(
                        ps[:, :], wo[:, c * 1024 + 128 * ob:c * 1024 + 128 * (ob + 1)],
                        ctx[0][:, c * T + BW * i:c * T + BW * (i + 1)],
                        start=(c == 0), stop=(c == NC8 - 1), skip_group_check=True),
                        reads=[ctx[1], wo_b], writes=[ps_b])
                p.emit("vector", lambda e, ps=ps: e.tensor_copy(sout[0][:, :], ps[:, :]),
                       reads=[ps_b], writes=[sout[1]])
                self.store(out_h[128 * ob:128 * (ob + 1), BW * i:BW * (i + 1)], sout[0][:, :], sout[1], wbuf=out_b)

    def emit_l1_moe_and_gate(self, xn16_name, x_name, s_name, x2_name):
        p, nc = self.p, self.nc
        self.p.new_phase()
        xn_h, xn_b = self.d(xn16_name)
        x_h, x_b = self.d(x_name)
        s_h, s_b = self.d(s_name)
        x2_h, x2_b = self.d(x2_name)
        w1q_h, w1q_b = self.d("l1_w1q")
        w2q_h, w2q_b = self.d("l1_w2q")
        ht_h, ht_b = self.d("l1_ht")
        mk_h, mk_b = self.d("l1_mask")
        ones_row, ones_row_b = p.tile("ones_row", [1, 128], F32)
        ident, ident_b = p.tile("ident_sb", [128, 128], F32)
        cvb = self.cvec_buf()

        w1t_h, w1t_db = self.d("l1_w1t")
        w2t_h, w2t_db = self.d("l1_w2t")
        stg = p.tile("m_stg", [128, 4096], F32)
        stg16 = p.tile("m_stg16", [128, 4096], BF16)
        for e in range(8):
            for c in range(NC8):
                self.load(stg[0][:, :], w1t_h[e, 128 * c:128 * (c + 1), :], stg[1], rbuf=w1t_db)
                p.emit("vector", lambda e2: e2.tensor_copy(stg16[0][:, :], stg[0][:, :]),
                       reads=[stg[1]], writes=[stg16[1]])
                self.store(w1q_h[e, 128 * c:128 * (c + 1), :], stg16[0][:, :], stg16[1], wbuf=w1q_b)
            for fc in range(32):
                self.load(stg[0][:, 0:1024], w2t_h[e, 128 * fc:128 * (fc + 1), :], stg[1], rbuf=w2t_db)
                p.emit("vector", lambda e2: e2.tensor_copy(stg16[0][:, 0:1024], stg[0][:, 0:1024]),
                       reads=[stg[1]], writes=[stg16[1]])
                self.store(w2q_h[e, 128 * fc:128 * (fc + 1), :], stg16[0][:, 0:1024], stg16[1], wbuf=w2q_b)

        xnr = p.tile("m_xn", [128, 8 * T], BF16)
        for c in range(NC8):
            self.load(xnr[0][:, c * T:(c + 1) * T], xn_h[128 * c:128 * (c + 1), :], xnr[1], rbuf=xn_b)

        wr_h, wr_db = self.d("l1_wr_t")
        br_h, br_db = self.d("l1_br")
        wr, wr_b = p.tile("m_wr", [128, 64], BF16)
        wrs = p.tile("m_wrs", [128, 64], F32)
        br, br_b = p.tile("m_br", [8, 1], F32)
        self.load(wrs[0][:, 0:64], AP(wr_h, 0, [[8, 128], [1024, 8], [1, 8]]), wrs[1], rbuf=wr_db)
        p.emit("vector", lambda e: e.tensor_copy(wr[:, :], wrs[0][:, :]), reads=[wrs[1]], writes=[wr_b])
        self.load(br[:, :], br_h[:, :], br_b, rbuf=br_db)
        lg = p.tile("m_lg", [8, BW], F32)
        lgtm = p.tile("m_lgtm", [128, 8], F32)
        mx = p.tile("m_mx", [128, 1], F32)
        eq = p.tile("m_eq", [128, 8], F32)
        cs1 = p.tile("m_cs1", [128, 8], F32)
        cs2 = p.tile("m_cs2", [128, 8], F32)
        oh = p.tile("m_oh", [128, 8], F32)
        mske = p.tile("m_mske", [8, 128], F32)
        for nb in range(NBLK):
            ps, ps_b = self.pb(0)
            for c in range(NC8):
                p.emit("tensor", lambda e, c=c, nb=nb, ps=ps: e.matmul(
                    ps[0:8, :], wr[:, 8 * c:8 * (c + 1)], xnr[0][:, c * T + nb * BW:c * T + (nb + 1) * BW],
                    start=(c == 0), stop=(c == NC8 - 1), skip_group_check=True),
                    reads=[xnr[1], wr_b], writes=[ps_b])
            p.emit("scalar", lambda e, ps=ps: e.activation(lg[0][0:8, :], ps[0:8, :], AF.Identity, bias=br[0:8, :]),
                   reads=[ps_b, br_b], writes=[lg[1]])
            for sblk in range(4):
                ps2, ps2_b = self.pb(1)
                p.emit("tensor", lambda e, sblk=sblk, ps2=ps2: e.transpose(
                    ps2[:, 0:8], lg[0][0:8, 128 * sblk:128 * (sblk + 1)], ident[0:8, 0:8]),
                    reads=[lg[1], ident_b], writes=[ps2_b])
                p.emit("vector", lambda e, ps2=ps2: e.tensor_copy(lgtm[0][:, :], ps2[:, 0:8]),
                       reads=[ps2_b], writes=[lgtm[1]])
                p.emit("vector", lambda e: e.reduce_max(mx[0][:, :], lgtm[0][:, :], axis=AX.X),
                       reads=[lgtm[1]], writes=[mx[1]])
                p.emit("vector", lambda e: e.tensor_scalar(eq[0][:, :], lgtm[0][:, :], mx[0][:, 0:1], None, ALU.is_ge),
                       reads=[lgtm[1], mx[1]], writes=[eq[1]])
                p.emit("vector", lambda e: e.tensor_copy(cs1[0][:, :], eq[0][:, :]), reads=[eq[1]], writes=[cs1[1]])
                for stp, (src, dst) in zip((1, 2, 4), ((cs1, cs2), (cs2, cs1), (cs1, cs2))):
                    p.emit("vector", lambda e, stp=stp, src=src, dst=dst: e.tensor_tensor(
                        dst[0][:, stp:8], src[0][:, stp:8], src[0][:, 0:8 - stp], ALU.add),
                        reads=[src[1]], writes=[dst[1]])
                    p.emit("vector", lambda e, stp=stp, src=src, dst=dst: e.tensor_copy(
                        dst[0][:, 0:stp], src[0][:, 0:stp]),
                        reads=[src[1]], writes=[dst[1]])
                p.emit("vector", lambda e: e.tensor_scalar(cs1[0][:, :], cs2[0][:, :], 1.0, None, ALU.is_le),
                       reads=[cs2[1]], writes=[cs1[1]])
                p.emit("vector", lambda e: e.tensor_tensor(oh[0][:, :], eq[0][:, :], cs1[0][:, :], ALU.mult),
                       reads=[eq[1], cs1[1]], writes=[oh[1]])
                ps3, ps3_b = self.pb(2)
                p.emit("tensor", lambda e, ps3=ps3: e.transpose(ps3[0:8, 0:128], oh[0][:, :], ident[:, :]),
                       reads=[oh[1], ident_b], writes=[ps3_b])
                p.emit("vector", lambda e, ps3=ps3: e.tensor_copy(mske[0][0:8, :], ps3[0:8, 0:128]),
                       reads=[ps3_b], writes=[mske[1]])
                self.store(mk_h[0:8, nb * BW + 128 * sblk:nb * BW + 128 * (sblk + 1)], mske[0][0:8, :], mske[1], wbuf=mk_b)

        b1_h, b1_db = self.d("l1_b1cm")
        b1, b1_b = p.tile("m_b1", [128, 256], F32)
        self.load(b1[:, :], b1_h[:, :], b1_b, rbuf=b1_db)
        w1sl = p.tile("m_w1sl", [128, 8 * 128], BF16)
        mrow = p.tile("m_mrow", [1, BW], F32)
        relu_t = p.tile("m_relu", [128, BW], F32)
        ht16 = p.tile("m_ht16", [128, BW], BF16)
        for e in range(8):
            for ft in range(32):
                self.load(w1sl[0][:, :],
                          AP(w1q_h, e * 1024 * 4096 + 128 * ft,
                             [[4096, 128], [128 * 4096, 8], [1, 128]]),
                          w1sl[1], rbuf=w1q_b)
                for tokb in range(NBLK):
                    ps, ps_b = self.pb(3 + ft % 2)
                    for c in range(NC8):
                        p.emit("tensor", lambda e2, c=c, tokb=tokb, ps=ps: e2.matmul(
                            ps[:, :], w1sl[0][:, 128 * c:128 * (c + 1)],
                            xnr[0][:, c * T + tokb * BW:c * T + (tokb + 1) * BW],
                            start=(c == 0), stop=(c == NC8 - 1), skip_group_check=True),
                            reads=[w1sl[1], xnr[1]], writes=[ps_b])
                    p.emit("scalar", lambda e2, e=e, ft=ft, ps=ps: e2.activation(
                        relu_t[0][:, :], ps[:, :], AF.Relu, bias=b1[:, e * 32 + ft:e * 32 + ft + 1]),
                        reads=[ps_b, b1_b], writes=[relu_t[1]])
                    self.load(mrow[0][0:1, :], mk_h[e:e + 1, tokb * BW:(tokb + 1) * BW], mrow[1], rbuf=mk_b)
                    ps4, ps4_b = self.pb(5)
                    p.emit("tensor", lambda e2, ps4=ps4: e2.matmul(
                        ps4[:, :], ones_row[0:1, :], mrow[0][0:1, :],
                        start=True, stop=True, skip_group_check=True),
                        reads=[mrow[1], ones_row_b], writes=[ps4_b])
                    p.emit("vector", lambda e2, ps4=ps4: e2.tensor_tensor(
                        ht16[0][:, :], relu_t[0][:, :], ps4[:, :], ALU.mult),
                        reads=[relu_t[1], ps4_b], writes=[ht16[1]])
                    self.store(ht_h[e, 128 * ft:128 * (ft + 1), tokb * BW:(tokb + 1) * BW],
                               ht16[0][:, :], ht16[1], wbuf=ht_b)

        b2_h, b2_db = self.d("l1_b2")
        b2s = p.tile("m_b2s", [8, 1024], F32)
        b2, b2_b = p.tile("m_b2", [8, 1024], BF16)
        self.load(b2s[0][0:8, :], b2_h[:, :], b2s[1], rbuf=b2_db)
        p.emit("vector", lambda e: e.tensor_copy(b2[0:8, :], b2s[0][0:8, :]), reads=[b2s[1]], writes=[b2_b])
        msk8 = p.tile("m_msk8", [8, BW], F32)
        msk8b = p.tile("m_msk8b", [8, BW], BF16)
        htl = p.tile("m_htl", [128, BW], BF16)
        w2row = p.tile("m_w2row", [128, 1024], BF16)
        e1 = p.tile("m_e1", [128, BW], F32)
        gt = p.tile("m_gt", [128, BW], F32)
        xg = p.tile("m_xg", [128, BW], F32)
        sg = p.tile("m_sg", [128, BW], F32)
        for tokb in range(NBLK):
            self.load(msk8[0][0:8, :], mk_h[0:8, tokb * BW:(tokb + 1) * BW], msk8[1], rbuf=mk_b)
            p.emit("vector", lambda e: e.tensor_copy(msk8b[0][0:8, :], msk8[0][0:8, :]),
                   reads=[msk8[1]], writes=[msk8b[1]])
            ybanks = [self.pb(d) for d in range(8)]
            for d in range(8):
                ps, ps_b = ybanks[d]
                p.emit("tensor", lambda e, d=d, ps=ps: e.matmul(
                    ps[:, :], b2[0:8, 128 * d:128 * (d + 1)], msk8b[0][0:8, :],
                    start=True, stop=False, skip_group_check=True),
                    reads=[b2_b, msk8b[1]], writes=[ps_b])
            for e in range(8):
                for ft in range(32):
                    self.load(htl[0][:, :], ht_h[e, 128 * ft:128 * (ft + 1), tokb * BW:(tokb + 1) * BW],
                              htl[1], rbuf=ht_b)
                    self.load(w2row[0][:, :], w2q_h[e, 128 * ft:128 * (ft + 1), :], w2row[1], rbuf=w2q_b)
                    for d in range(8):
                        ps, ps_b = ybanks[d]
                        p.emit("tensor", lambda e2, e=e, ft=ft, d=d, ps=ps: e2.matmul(
                            ps[:, :], w2row[0][:, 128 * d:128 * (d + 1)], htl[0][:, :],
                            start=False, stop=(e == 7 and ft == 31), skip_group_check=True),
                            reads=[w2row[1], htl[1]], writes=[ps_b])
            for d in range(8):
                ps, ps_b = ybanks[d]
                p.emit("scalar", lambda e, ps=ps: e.activation(e1[0][:, :], ps[:, :], AF.Exp, scale=self.cap(-1.0)),
                       reads=[ps_b, cvb], writes=[e1[1]])
                p.emit("vector", lambda e: e.tensor_scalar(e1[0][:, :], e1[0][:, :], 1.0, None, ALU.add),
                       reads=[], writes=[e1[1]])
                p.emit("vector", lambda e: e.reciprocal(e1[0][:, :], e1[0][:, :]),
                       reads=[], writes=[e1[1]])
                p.emit("vector", lambda e, ps=ps: e.tensor_tensor(gt[0][:, :], ps[:, :], e1[0][:, :], ALU.mult),
                       reads=[ps_b, e1[1]], writes=[gt[1]])
                self.load(xg[0][:, :], x_h[128 * d:128 * (d + 1), tokb * BW:(tokb + 1) * BW], xg[1], rbuf=x_b)
                self.load(sg[0][:, :], s_h[128 * d:128 * (d + 1), tokb * BW:(tokb + 1) * BW], sg[1], rbuf=s_b)
                p.emit("vector", lambda e: e.tensor_tensor(gt[0][:, :], gt[0][:, :], sg[0][:, :], ALU.mult),
                       reads=[sg[1]], writes=[gt[1]])
                p.emit("vector", lambda e: e.tensor_tensor(gt[0][:, :], gt[0][:, :], xg[0][:, :], ALU.add),
                       reads=[xg[1]], writes=[gt[1]])
                self.store(x2_h[128 * d:128 * (d + 1), tokb * BW:(tokb + 1) * BW], gt[0][:, :], gt[1], wbuf=x2_b)

    def emit_lna(self, src_name, dst16_name, gb, gcol, bcol, dst32_name=None, pfx="lna"):
        p = self.p
        src_h, src_b = self.d(src_name)
        d16_h, d16_b = self.d(dst16_name)
        d32 = self.d(dst32_name) if dst32_name else None
        ones_col, ones_col_b = p.tile("ones_col", [128, 1], F32)
        ones_row, ones_row_b = p.tile("ones_row", [1, 128], F32)
        gbt, gb_b = gb
        st = p.tile(pfx + "_small", [1, 3 * BW], F32)
        for nb in range(NBLK):
            xc = []
            for c in range(NC8):
                t, b = p.tile(f"{pfx}_x{c}", [128, BW], F32)
                self.load(t[:, :], src_h[128 * c:128 * (c + 1), nb * BW:(nb + 1) * BW], b, rbuf=src_b)
                xc.append((t, b))
            sq = []
            for c in range(NC8):
                t, b = p.tile(f"{pfx}_o{c}", [128, BW], F32)
                p.emit("scalar", lambda e, t=t, s=xc[c][0]: e.activation(t[:, :], s[:, :], AF.Square),
                       reads=[xc[c][1]], writes=[b])
                sq.append((t, b))
            psm, psm_b = self.pb(0)
            pss, pss_b = self.pb(1)
            for c in range(NC8):
                p.emit("tensor", lambda e, c=c: e.matmul(
                    psm[0:1, :], ones_col[:, :], xc[c][0][:, :],
                    start=(c == 0), stop=(c == NC8 - 1), skip_group_check=True),
                    reads=[xc[c][1], ones_col_b], writes=[psm_b])
            for c in range(NC8):
                p.emit("tensor", lambda e, c=c: e.matmul(
                    pss[0:1, :], ones_col[:, :], sq[c][0][:, :],
                    start=(c == 0), stop=(c == NC8 - 1), skip_group_check=True),
                    reads=[sq[c][1], ones_col_b], writes=[pss_b])
            mean = st[0][0:1, 0:BW]
            var = st[0][0:1, BW:2 * BW]
            rstd = st[0][0:1, 2 * BW:3 * BW]
            p.emit("vector", lambda e: e.tensor_scalar(mean, psm[0:1, :], 1.0 / D, None, ALU.mult),
                   reads=[psm_b], writes=[st[1]])
            p.emit("vector", lambda e: e.tensor_tensor(rstd, mean, mean, ALU.mult),
                   reads=[], writes=[st[1]])
            p.emit("vector", lambda e: e.scalar_tensor_tensor(var, pss[0:1, :], 1.0 / D, rstd, ALU.mult, ALU.subtract),
                   reads=[pss_b], writes=[st[1]])
            p.emit("scalar", lambda e: e.activation(rstd, var, AF.Ln, bias=self.cap(EPS_LN, 1)),
                   reads=[self.cvec_buf()], writes=[st[1]])
            p.emit("scalar", lambda e: e.activation(rstd, rstd, AF.Exp, scale=self.cap(-0.5, 1)),
                   reads=[self.cvec_buf()], writes=[st[1]])
            pmb, pmb_b = self.pb(2)
            prb, prb_b = self.pb(3)
            p.emit("tensor", lambda e: e.matmul(pmb[:, :], ones_row[:, :], mean, start=True, stop=True,
                                                skip_group_check=True),
                   reads=[st[1], ones_row_b], writes=[pmb_b])
            p.emit("tensor", lambda e: e.matmul(prb[:, :], ones_row[:, :], rstd, start=True, stop=True,
                                                skip_group_check=True),
                   reads=[st[1], ones_row_b], writes=[prb_b])
            for c in range(NC8):
                t32, b32 = sq[c]
                t16, b16 = p.tile(f"{pfx}_b{c}", [128, BW], BF16)
                p.emit("vector", lambda e, c=c, t32=t32: e.tensor_tensor(
                    t32[:, :], xc[c][0][:, :], pmb[:, :], ALU.subtract),
                    reads=[xc[c][1], pmb_b], writes=[b32])
                p.emit("vector", lambda e, c=c, t32=t32: e.tensor_tensor(
                    t32[:, :], t32[:, :], prb[:, :], ALU.mult),
                    reads=[prb_b], writes=[b32])
                if gbt is not None:
                    p.emit("vector", lambda e, c=c, t32=t32: e.tensor_scalar(
                        t32[:, :], t32[:, :], gbt[:, gcol + c:gcol + c + 1], gbt[:, bcol + c:bcol + c + 1],
                        ALU.mult, ALU.add),
                        reads=[gb_b], writes=[b32])
                p.emit("scalar", lambda e, t32=t32, t16=t16: e.activation(t16[:, :], t32[:, :], AF.Identity),
                       reads=[b32], writes=[b16])
                self.store(d16_h[128 * c:128 * (c + 1), nb * BW:(nb + 1) * BW], t16[:, :], b16, wbuf=d16_b)
                if d32 is not None:
                    self.store(d32[0][128 * c:128 * (c + 1), nb * BW:(nb + 1) * BW], t32[:, :], b32, wbuf=d32[1])

    def emit_l2_gbranch(self, xn32_name):
        p = self.p
        xn_h, xn_b = self.d(xn32_name)
        g_h, g_b = self.d("l2_g")
        ones_row, ones_row_b = p.tile("ones_row", [1, 128], F32)
        ident, ident_b = p.tile("ident_sb", [128, 128], F32)
        cvb = self.cvec_buf()
        ws1_h, ws1_db = self.d("l2_ws1")
        ws2_h, ws2_db = self.d("l2_ws2")
        bs_h, bs_db = self.d("l2_bs")
        wqkvo_h, wqkvo_db = self.d("l2f_wqkvo")
        ltm_h, ltm_db = self.d("l2_ltmask")
        osel_h, osel_db = self.d("l2_ownsel")

        gm = p.tile("g_gm", [128, 8 * 128], F32)
        gm16 = p.tile("g_gm16", [128, 8 * 128], BF16)
        row = p.tile("g_row", [128, 2048], F32)
        half = p.tile("g_half", [128, 1024], F32)
        for c in range(NC8):
            self.load(row[0][:, :], xn_h[128 * c:128 * (c + 1), :], row[1], rbuf=xn_b)
            p.emit("vector", lambda e: e.tensor_tensor(half[0][:, 0:1024], row[0][:, 0:2048:2], row[0][:, 1:2048:2], ALU.add),
                   reads=[row[1]], writes=[half[1]])
            p.emit("vector", lambda e: e.tensor_tensor(row[0][:, 0:512], half[0][:, 0:1024:2], half[0][:, 1:1024:2], ALU.add),
                   reads=[half[1]], writes=[row[1]])
            p.emit("vector", lambda e: e.tensor_tensor(half[0][:, 0:256], row[0][:, 0:512:2], row[0][:, 1:512:2], ALU.add),
                   reads=[row[1]], writes=[half[1]])
            p.emit("vector", lambda e, c=c: e.tensor_tensor(row[0][:, 0:128], half[0][:, 0:256:2], half[0][:, 1:256:2], ALU.add),
                   reads=[half[1]], writes=[row[1]])
            p.emit("vector", lambda e, c=c: e.tensor_scalar(gm[0][:, 128 * c:128 * (c + 1)], row[0][:, 0:128], 1.0 / 16.0, None, ALU.mult),
                   reads=[row[1]], writes=[gm[1]])
            p.emit("vector", lambda e, c=c: e.tensor_copy(gm16[0][:, 128 * c:128 * (c + 1)], gm[0][:, 128 * c:128 * (c + 1)]),
                   reads=[gm[1]], writes=[gm16[1]])

        ws1 = p.tile("g_ws1", [128, 256], BF16)
        ws1s = p.tile("g_ws1s", [128, 256], F32)
        bs = p.tile("g_bs", [128, 4], F32)
        self.load(bs[0][:, 0:3], bs_h[:, :], bs[1], rbuf=bs_db)
        hh = p.tile("g_h", [128, 256], BF16)
        sc = p.tile("g_sc", [1, 256], F32)
        for c in range(NC8):
            self.load(ws1s[0][:, :], ws1_h[128 * c:128 * (c + 1), :], ws1s[1], rbuf=ws1_db)
            p.emit("vector", lambda e: e.tensor_copy(ws1[0][:, :], ws1s[0][:, :]), reads=[ws1s[1]], writes=[ws1[1]])
            for fb in range(2):
                ps, ps_b = self.pb(4 + fb)
                p.emit("tensor", lambda e, c=c, fb=fb, ps=ps: e.matmul(
                    ps[0:128, 0:128], ws1[0][:, 128 * fb:128 * (fb + 1)], gm16[0][:, 128 * c:128 * (c + 1)],
                    start=(c == 0), stop=(c == NC8 - 1), skip_group_check=True),
                    reads=[ws1[1], gm16[1]], writes=[ps_b])
        for fb in range(2):
            ps, ps_b = self.pb(4 + fb)
            p.emit("scalar", lambda e, fb=fb, ps=ps: e.activation(
                hh[0][:, 128 * fb:128 * (fb + 1)], ps[0:128, 0:128], AF.Relu, bias=bs[0][:, fb:fb + 1]),
                reads=[ps_b, bs[1]], writes=[hh[1]])
        ws2 = p.tile("g_ws2", [128, 2], BF16)
        ws2s = p.tile("g_ws2s", [128, 2], F32)
        self.load(ws2s[0][:, 0:1], ws2_h[0:128, :], ws2s[1], rbuf=ws2_db)
        self.load(ws2s[0][:, 1:2], ws2_h[128:256, :], ws2s[1], rbuf=ws2_db)
        p.emit("vector", lambda e: e.tensor_copy(ws2[0][:, :], ws2s[0][:, :]), reads=[ws2s[1]], writes=[ws2[1]])
        ps, ps_b = self.pb(6)
        for fb in range(2):
            p.emit("tensor", lambda e, fb=fb, ps=ps: e.matmul(
                ps[0:1, 0:128], ws2[0][:, fb:fb + 1], hh[0][:, 128 * fb:128 * (fb + 1)],
                start=(fb == 0), stop=(fb == 1), skip_group_check=True),
                reads=[ws2[1], hh[1]], writes=[ps_b])
        p.emit("scalar", lambda e, ps=ps: e.activation(sc[0][0:1, 0:128], ps[0:1, 0:128], AF.Identity,
                                                       bias=bs[0][0:1, 2:3]),
               reads=[ps_b, bs[1]], writes=[sc[1]])

        wql = p.tile("g_wql", [128, 8 * 1024], BF16)
        wqs = p.tile("g_wqs", [128, 1024], F32)

        def load_gw(widx):
            for c in range(NC8):
                self.load(wqs[0][:, :], wqkvo_h[widx, 128 * c:128 * (c + 1), :], wqs[1], rbuf=wqkvo_db)
                p.emit("vector", lambda e, c=c: e.tensor_copy(wql[0][:, 1024 * c:1024 * (c + 1)], wqs[0][:, :]),
                       reads=[wqs[1]], writes=[wql[1]])
        kg = p.tile("g_kg", [128, 8 * 128], BF16)
        vg = p.tile("g_vg", [128, 1040], BF16)
        load_gw(1)
        for ob in range(NC8):
            ps2, ps2_b = self.pb(4)
            for c in range(NC8):
                p.emit("tensor", lambda e, c=c, ob=ob, ps2=ps2: e.matmul(
                    ps2[0:128, 0:128], wql[0][:, 1024 * c + 128 * ob:1024 * c + 128 * (ob + 1)],
                    gm16[0][:, 128 * c:128 * (c + 1)],
                    start=(c == 0), stop=(c == NC8 - 1), skip_group_check=True),
                    reads=[wql[1], gm16[1]], writes=[ps2_b])
            p.emit("vector", lambda e, ob=ob, ps2=ps2: e.tensor_copy(kg[0][:, 128 * ob:128 * (ob + 1)], ps2[0:128, 0:128]),
                   reads=[ps2_b], writes=[kg[1]])
        p.emit("vector", lambda e: e.memset(vg[0][:, :], 1.0), reads=[], writes=[vg[1]])
        load_gw(2)
        for vb in range(2):
            ps2, ps2_b = self.pb(5)
            for c in range(NC8):
                p.emit("tensor", lambda e, c=c, vb=vb, ps2=ps2: e.matmul(
                    ps2[:, :], gm16[0][:, 128 * c:128 * (c + 1)],
                    wql[0][:, 1024 * c + 512 * vb:1024 * c + 512 * (vb + 1)],
                    start=(c == 0), stop=(c == NC8 - 1), skip_group_check=True),
                    reads=[wql[1], gm16[1]], writes=[ps2_b])
            p.emit("vector", lambda e, vb=vb, ps2=ps2: e.tensor_copy(
                vg[0][:, 65 * 8 * vb:65 * 8 * (vb + 1)].rearrange("p (h d) -> p h d", d=65)[:, :, 0:64],
                ps2[:, :].rearrange("p (h d) -> p h d", d=64)),
                reads=[ps2_b], writes=[vg[1]])

        scd_h, scd_b = self.d("l2_scd")
        scf_h, scf_b = self.d("l2_scf")
        gpk_h, gpk_b = self.d("l2_gpk")
        gpkf_h, gpkf_b = self.d("l2_gpkf")
        self.store(scd_h[0:1, :], sc[0][0:1, 0:128], sc[1], wbuf=scd_b)
        for c in range(NC8):
            self.store(AP(gpk_h, 128 * 128 * c, [[128, 128], [1, 128]]),
                       kg[0][:, 128 * c:128 * (c + 1)], kg[1], wbuf=gpk_b)
        self.store(AP(gpk_h, 131072, [[1040, 128], [1, 1040]]), vg[0][:, :], vg[1], wbuf=gpk_b)
        p.emit("gpsimd", lambda e: e.collective_compute(
            "AllGather", ALU.bypass, PAIRS, [scd_h.ap().opt()], [scf_h.ap().opt()]),
            reads=[scd_b], writes=[scf_b], kind="coll")
        p.emit("gpsimd", lambda e: e.collective_compute(
            "AllGather", ALU.bypass, PAIRS, [gpk_h.ap().opt()], [gpkf_h.ap().opt()]),
            reads=[gpk_b], writes=[gpkf_b], kind="coll")

        pr = p.tile("g_pr", [1, 256], F32)
        sm = p.tile("g_sm", [1, 4], F32)
        self.load(pr[0][0:1, :], AP(scf_h, 0, [[1, 1], [1, 256]]), pr[1], rbuf=scf_b)
        p.emit("vector", lambda e: e.reduce_max(sm[0][0:1, 0:1], pr[0][0:1, :], axis=AX.X),
               reads=[pr[1]], writes=[sm[1]])
        p.emit("vector", lambda e: e.tensor_scalar(sm[0][0:1, 1:2], sm[0][0:1, 0:1], -1.0, None, ALU.mult),
               reads=[], writes=[sm[1]])
        p.emit("scalar", lambda e: e.activation(pr[0][0:1, :], pr[0][0:1, :], AF.Exp, bias=sm[0][0:1, 1:2]),
               reads=[sm[1]], writes=[pr[1]])
        p.emit("vector", lambda e: e.reduce_sum(sm[0][0:1, 2:3], pr[0][0:1, :], axis=AX.X),
               reads=[], writes=[sm[1]])
        p.emit("vector", lambda e: e.reciprocal(sm[0][0:1, 3:4], sm[0][0:1, 2:3]),
               reads=[], writes=[sm[1]])
        p.emit("vector", lambda e: e.tensor_scalar(pr[0][0:1, :], pr[0][0:1, :], sm[0][0:1, 3:4], None, ALU.mult),
               reads=[sm[1]], writes=[pr[1]])
        prt = p.tile("g_prt", [128, 2], F32)
        for ch in range(2):
            ps3, ps3_b = self.pb(6)
            p.emit("tensor", lambda e, ch=ch, ps3=ps3: e.transpose(
                ps3[0:128, 0:1], pr[0][0:1, 128 * ch:128 * (ch + 1)], ident[0:1, 0:1]),
                reads=[pr[1], ident_b], writes=[ps3_b])
            p.emit("vector", lambda e, ch=ch, ps3=ps3: e.tensor_copy(prt[0][:, ch:ch + 1], ps3[0:128, 0:1]),
                   reads=[ps3_b], writes=[prt[1]])
        ltm = p.tile("g_ltm", [128, 2 * 256], F32)
        self.load(ltm[0][:, 0:256], ltm_h[0, :, :], ltm[1], rbuf=ltm_db)
        self.load(ltm[0][:, 256:512], ltm_h[1, :, :], ltm[1], rbuf=ltm_db)
        gt = p.tile("g_gt", [128, 256], F32)
        eqt = p.tile("g_eqt", [128, 256], F32)
        rnk = p.tile("g_rnk", [128, 2], F32)
        selc = p.tile("g_selc", [128, 2], F32)
        psA, psA_b = self.pb(6)
        p.emit("tensor", lambda e: e.matmul(psA[:, 0:256], ones_row[0:1, :], pr[0][0:1, :],
                                            start=True, stop=True, skip_group_check=True),
               reads=[pr[1], ones_row_b], writes=[psA_b])
        for ch in range(2):
            p.emit("vector", lambda e, ch=ch: e.tensor_scalar(
                gt[0][:, :], psA[:, 0:256], prt[0][:, ch:ch + 1], None, ALU.is_gt),
                reads=[psA_b, prt[1]], writes=[gt[1]])
            p.emit("vector", lambda e, ch=ch: e.tensor_scalar(
                eqt[0][:, :], psA[:, 0:256], prt[0][:, ch:ch + 1], None, ALU.is_equal),
                reads=[psA_b, prt[1]], writes=[eqt[1]])
            p.emit("vector", lambda e, ch=ch: e.tensor_tensor(
                eqt[0][:, :], eqt[0][:, :], ltm[0][:, 256 * ch:256 * (ch + 1)], ALU.mult),
                reads=[ltm[1]], writes=[eqt[1]])
            p.emit("vector", lambda e: e.tensor_tensor(gt[0][:, :], gt[0][:, :], eqt[0][:, :], ALU.add),
                   reads=[eqt[1]], writes=[gt[1]])
            p.emit("vector", lambda e, ch=ch: e.reduce_sum(rnk[0][:, ch:ch + 1], gt[0][:, :], axis=AX.X),
                   reads=[gt[1]], writes=[rnk[1]])
            p.emit("vector", lambda e, ch=ch: e.tensor_scalar(
                selc[0][:, ch:ch + 1], rnk[0][:, ch:ch + 1], 64.0, None, ALU.is_lt),
                reads=[rnk[1]], writes=[selc[1]])
        selT = p.tile("g_selT", [1, 256], F32)
        for ch in range(2):
            ps3, ps3_b = self.pb(6)
            p.emit("tensor", lambda e, ch=ch, ps3=ps3: e.transpose(
                ps3[0:1, 0:128], selc[0][:, ch:ch + 1], ident[:, :]),
                reads=[selc[1], ident_b], writes=[ps3_b])
            p.emit("vector", lambda e, ch=ch, ps3=ps3: e.tensor_copy(selT[0][0:1, 128 * ch:128 * (ch + 1)], ps3[0:1, 0:128]),
                   reads=[ps3_b], writes=[selT[1]])
        osel = p.tile("g_osel", [128, 256], F32)
        self.load(osel[0][:, 0:128], osel_h[0:128, :], osel[1], rbuf=osel_db)
        self.load(osel[0][:, 128:256], osel_h[128:256, :], osel[1], rbuf=osel_db)
        selownp, selownp_b = self.pb(6)
        stt = p.tile("g_selTT", [128, 2], F32)
        for ch in range(2):
            ps3, ps3_b = self.pb(5)
            p.emit("tensor", lambda e, ch=ch, ps3=ps3: e.transpose(
                ps3[0:128, 0:1], selT[0][0:1, 128 * ch:128 * (ch + 1)], ident[0:1, 0:1]),
                reads=[selT[1], ident_b], writes=[ps3_b])
            p.emit("vector", lambda e, ch=ch, ps3=ps3: e.tensor_copy(stt[0][:, ch:ch + 1], ps3[0:128, 0:1]),
                   reads=[ps3_b], writes=[stt[1]])
        for ch in range(2):
            p.emit("tensor", lambda e, ch=ch: e.matmul(
                selownp[0:1, 0:128], stt[0][:, ch:ch + 1], osel[0][:, 128 * ch:128 * (ch + 1)],
                start=(ch == 0), stop=(ch == 1), skip_group_check=True),
                reads=[stt[1], osel[1]], writes=[selownp_b])
        selown = p.tile("g_selown", [1, 128], F32)
        p.emit("vector", lambda e: e.tensor_copy(selown[0][0:1, :], selownp[0:1, 0:128]),
               reads=[selownp_b], writes=[selown[1]])

        qg = p.tile("g_qg", [128, 8 * 128], BF16)
        load_gw(0)
        for ob in range(NC8):
            ps2, ps2_b = self.pb(4)
            for c in range(NC8):
                p.emit("tensor", lambda e, c=c, ob=ob, ps2=ps2: e.matmul(
                    ps2[0:128, 0:128], wql[0][:, 1024 * c + 128 * ob:1024 * c + 128 * (ob + 1)],
                    gm16[0][:, 128 * c:128 * (c + 1)],
                    start=(c == 0), stop=(c == NC8 - 1), skip_group_check=True),
                    reads=[wql[1], gm16[1]], writes=[ps2_b])
            p.emit("vector", lambda e, ob=ob, ps2=ps2: e.tensor_copy(qg[0][:, 128 * ob:128 * (ob + 1)], ps2[0:128, 0:128]),
                   reads=[ps2_b], writes=[qg[1]])
        kf = p.tile("g_kf", [128, 16 * 128], BF16)
        for ch in range(16):
            r_, cc_ = divmod(ch, 8)
            self.load(kf[0][:, 128 * ch:128 * (ch + 1)],
                      AP(gpkf_h, r_ * 264192 + 128 * 128 * cc_, [[128, 128], [1, 128]]),
                      kf[1], rbuf=gpkf_b)
        vf = p.tile("g_vf", [128, 2 * 1040], BF16)
        for ch in range(2):
            self.load(vf[0][:, 1040 * ch:1040 * (ch + 1)],
                      AP(gpkf_h, ch * 264192 + 131072, [[1040, 128], [1, 1040]]),
                      vf[1], rbuf=gpkf_b)
        ctxg = p.tile("g_ctx", [128, 8 * 128], BF16)
        ptile = p.tile("g_pt", [128, 128], BF16)
        recg = p.tile("g_rec", [1, 128], F32)
        ytm = p.tile("g_ytm", [128, 128], F32)
        for h in range(16):
            pv, pv_b = self.pb(5)
            for kch in range(2):
                ps2, ps2_b = self.pb(4)
                kcol = kch * 8 + (64 * h) // 128
                kbase_ = (64 * h) % 128
                p.emit("tensor", lambda e, h=h, kch=kch, ps2=ps2, kcol=kcol, kbase_=kbase_: e.matmul(
                    ps2[0:128, 0:128], kf[0][kbase_:kbase_ + 64, 128 * kcol:128 * (kcol + 1)],
                    qg[0][kbase_:kbase_ + 64, 128 * ((64 * h) // 128):128 * ((64 * h) // 128) + 128],
                    start=True, stop=True, skip_group_check=True, tile_position=(kbase_, 0)),
                    reads=[kf[1], qg[1]], writes=[ps2_b])
                p.emit("scalar", lambda e, ps2=ps2: e.activation(
                    ptile[0][:, :], ps2[0:128, 0:128], AF.Exp, scale=self.cap(0.125)),
                    reads=[ps2_b, cvb], writes=[ptile[1]])
                p.emit("vector", lambda e, kch=kch: e.tensor_scalar(
                    ptile[0][:, :], ptile[0][:, :], selc[0][:, kch:kch + 1], None, ALU.mult),
                    reads=[selc[1]], writes=[ptile[1]])
                p.emit("tensor", lambda e, h=h, kch=kch, pv=pv: e.matmul(
                    pv[0:65, 0:128], vf[0][:, 1040 * kch + 65 * h:1040 * kch + 65 * (h + 1)],
                    ptile[0][:, :], start=(kch == 0), stop=(kch == 1), skip_group_check=True),
                    reads=[vf[1], ptile[1]], writes=[pv_b])
            p.emit("vector", lambda e, pv=pv: e.reciprocal(recg[0][0:1, :], pv[64:65, 0:128]),
                   reads=[pv_b], writes=[recg[1]])
            psb, psb_b = self.pb(6)
            p.emit("tensor", lambda e, psb=psb: e.matmul(
                psb[0:64, 0:128], ones_row[0:1, 0:64], recg[0][0:1, :],
                start=True, stop=True, skip_group_check=True),
                reads=[recg[1], ones_row_b], writes=[psb_b])
            p.emit("vector", lambda e, pv=pv: e.tensor_copy(ytm[0][0:64, :], pv[0:64, 0:128]),
                   reads=[pv_b], writes=[ytm[1]])
            p.emit("vector", lambda e, h=h, psb=psb: e.tensor_tensor(
                ctxg[0][64 * (h % 2):64 * (h % 2) + 64, 128 * (h // 2):128 * (h // 2) + 128],
                ytm[0][0:64, :], psb[0:64, 0:128], ALU.mult),
                reads=[ytm[1], psb_b], writes=[ctxg[1]])
        gsm = p.tile("g_gsm", [128, 128], F32)
        e1g = p.tile("g_e1", [128, 128], F32)
        gful = p.tile("g_gful", [128, 2048], F32)
        selb, selb_b = self.pb(6)
        p.emit("tensor", lambda e: e.matmul(selb[:, 0:128], ones_row[0:1, :], selown[0][0:1, :],
                                            start=True, stop=True, skip_group_check=True),
               reads=[selown[1], ones_row_b], writes=[selb_b])
        selbs = p.tile("g_selbs", [128, 128], F32)
        p.emit("vector", lambda e: e.tensor_copy(selbs[0][:, :], selb[:, 0:128]),
               reads=[selb_b], writes=[selbs[1]])
        load_gw(3)
        for ob in range(NC8):
            ps2, ps2_b = self.pb(4)
            for c in range(NC8):
                p.emit("tensor", lambda e, c=c, ob=ob, ps2=ps2: e.matmul(
                    ps2[0:128, 0:128], wql[0][:, 1024 * c + 128 * ob:1024 * c + 128 * (ob + 1)],
                    ctxg[0][:, 128 * c:128 * (c + 1)],
                    start=(c == 0), stop=(c == NC8 - 1), skip_group_check=True),
                    reads=[wql[1], ctxg[1]], writes=[ps2_b])
            p.emit("vector", lambda e, ps2=ps2: e.tensor_tensor(gsm[0][:, :], ps2[0:128, 0:128], selbs[0][:, :], ALU.mult),
                   reads=[ps2_b, selbs[1]], writes=[gsm[1]])
            p.emit("scalar", lambda e: e.activation(e1g[0][:, :], gsm[0][:, :], AF.Exp, scale=self.cap(-1.0)),
                   reads=[gsm[1], cvb], writes=[e1g[1]])
            p.emit("vector", lambda e: e.tensor_scalar(e1g[0][:, :], e1g[0][:, :], 1.0, None, ALU.add),
                   reads=[], writes=[e1g[1]])
            p.emit("vector", lambda e: e.reciprocal(e1g[0][:, :], e1g[0][:, :]),
                   reads=[], writes=[e1g[1]])
            p.emit("vector", lambda e: e.tensor_tensor(gsm[0][:, :], gsm[0][:, :], e1g[0][:, :], ALU.mult),
                   reads=[e1g[1]], writes=[gsm[1]])
            p.emit("vector", lambda e: e.tensor_copy(gful[0][:, 0:2048:16], gsm[0][:, :]),
                   reads=[gsm[1]], writes=[gful[1]])
            for stp in (16, 8, 4, 2):
                p.emit("vector", lambda e, stp=stp: e.tensor_copy(
                    gful[0][:, stp // 2:2048:stp], gful[0][:, 0:2048:stp]),
                    reads=[], writes=[gful[1]])
            self.store(g_h[128 * ob:128 * (ob + 1), :], gful[0][:, :], gful[1], wbuf=g_b)

    def emit_l2_sbranch(self, xn32_name, x_name, x3_name):
        p = self.p
        x_h, x_b = self.d(x_name)
        x3_h, x3_b = self.d(x3_name)
        g_h, g_b = self.d("l2_g")
        z2_h, z2_b = self.d("l2_z2")
        z4_h, z4_b = self.d("l2_z4")
        z3f_h, z3f_b = self.d("l2_z3f")
        k_h, k_b = self.d("l2_k")
        v_h, v_b = self.d("l2_v")
        q_h, q_b = self.d("l2_q")
        ones_row, ones_row_b = p.tile("ones_row", [1, 128], F32)
        cvb = self.cvec_buf()

        def lnp():
            t, b = p.tile("l2_lnp_sb", [128, 48], F32)
            return (t, b)

        self.p.new_phase()
        lp = lnp()
        lp_h, lp_db = self.d("l2_lnp")
        self.load(lp[0][:, :], lp_h[:, :], lp[1], rbuf=lp_db)
        self.emit_lna(xn32_name, "l2_z1", lp, 0, 8, pfx="lna")

        self.p.new_phase()
        lp = lnp()
        self.load(lp[0][:, :], lp_h[:, :], lp[1], rbuf=lp_db)
        sfw_h, sfw_db = self.d("l2_sfw_t")
        msk_h, msk_db = self.d("l2_mask_t")
        bias_h, bias_db = self.d("l2_bias")
        bias, bias_b = p.tile("l2_bias_sb", [128, 48], F32)
        self.load(bias[:, :], bias_h[:, :], bias_b, rbuf=bias_db)
        wsf = p.tile("s_wsf", [128, 8 * 1024], BF16)
        wstg = p.tile("s_wstg", [128, 1024], F32)
        wstg2 = p.tile("s_wstg2", [128, 1024], F32)
        for c in range(NC8):
            self.load(wstg[0][:, :], sfw_h[128 * c:128 * (c + 1), :], wstg[1], rbuf=sfw_db)
            self.load(wstg2[0][:, :], msk_h[128 * c:128 * (c + 1), :], wstg2[1], rbuf=msk_db)
            p.emit("vector", lambda e, c=c: e.tensor_tensor(
                wsf[0][:, 1024 * c:1024 * (c + 1)], wstg[0][:, :], wstg2[0][:, :], ALU.mult),
                reads=[wstg[1], wstg2[1]], writes=[wsf[1]])
        z1_h, z1_db = self.d("l2_z1")
        zt = [p.tile(f"s_z{c}", [128, BW], BF16) for c in range(NC8)]
        tev = p.tile("s_tev", [128, BW], F32)
        xnld = p.tile("s_xnld", [128, BW], F32)
        xn32_h, xn32_db = self.d(xn32_name)
        for nb in range(NBLK):
            for c in range(NC8):
                t, b = zt[c]
                self.load(t[:, :], z1_h[128 * c:128 * (c + 1), nb * BW:(nb + 1) * BW], b, rbuf=z1_db)
            for ob in range(NC8):
                ps, ps_b = self.pb(4 + ob % 2)
                for c in range(NC8):
                    p.emit("tensor", lambda e, c=c, ob=ob, ps=ps: e.matmul(
                        ps[:, :], wsf[0][:, 1024 * c + 128 * ob:1024 * c + 128 * (ob + 1)],
                        zt[c][0][:, :], start=(c == 0), stop=(c == NC8 - 1), skip_group_check=True),
                        reads=[zt[c][1], wsf[1]], writes=[ps_b])
                p.emit("scalar", lambda e, ob=ob, ps=ps: e.activation(
                    tev[0][:, :], ps[:, :], AF.Identity, bias=bias[:, 32 + ob:33 + ob]),
                    reads=[ps_b, bias_b], writes=[tev[1]])
                self.load(xnld[0][:, :], xn32_h[128 * ob:128 * (ob + 1), nb * BW:(nb + 1) * BW], xnld[1], rbuf=xn32_db)
                p.emit("vector", lambda e: e.tensor_tensor(tev[0][:, :], tev[0][:, :], xnld[0][:, :], ALU.add),
                       reads=[xnld[1]], writes=[tev[1]])
                self.store(z2_h[128 * ob:128 * (ob + 1), nb * BW:(nb + 1) * BW], tev[0][:, :], tev[1], wbuf=z2_b)

        self.p.new_phase()
        lp = lnp()
        self.load(lp[0][:, :], lp_h[:, :], lp[1], rbuf=lp_db)
        self.emit_lna("l2_z2", "l2_z3", lp, 16, 24, pfx="lna")
        z3_h, z3_db = self.d("l2_z3")
        p.emit("gpsimd", lambda e: e.collective_compute(
            "AllGather", ALU.bypass, PAIRS, [z3_h.ap().opt()], [z3f_h.ap().opt()]),
            reads=[z3_db], writes=[z3f_b], kind="coll")

        self.p.new_phase()
        bias4, bias4_b = p.tile("l2_bias_sb", [128, 48], F32)
        self.load(bias4[:, :], bias_h[:, :], bias4_b, rbuf=bias_db)
        wqkv_h, wqkv_db = self.d("l2_wqkv_t")
        bvr_h, bvr_db = self.d("l2_bv_row")
        bvr = p.tile("s_bvr", [1, 1024], BF16)
        bvrs = p.tile("s_bvrs", [1, 1024], F32)
        o16, o16_b = p.tile("ones_col16", [128, 1], BF16)
        ones_row16 = p.tile("s_ones16", [1, 128], BF16)
        p.emit("vector", lambda e: e.memset(ones_row16[0][0:1, :], 1.0), reads=[], writes=[ones_row16[1]])
        self.load(bvrs[0][0:1, :], bvr_h[:, :], bvrs[1], rbuf=bvr_db)
        p.emit("vector", lambda e: e.tensor_copy(bvr[0][0:1, :], bvrs[0][0:1, :]), reads=[bvrs[1]], writes=[bvr[1]])
        wql = p.tile("s_wql", [128, 8 * 1024], BF16)
        wstg4 = p.tile("s_wstg", [128, 1024], F32)

        def load_w2(widx):
            for c in range(NC8):
                self.load(wstg4[0][:, :], wqkv_h[widx, 128 * c:128 * (c + 1), :], wstg4[1], rbuf=wqkv_db)
                p.emit("vector", lambda e, c=c: e.tensor_copy(wql[0][:, 1024 * c:1024 * (c + 1)], wstg4[0][:, :]),
                       reads=[wstg4[1]], writes=[wql[1]])

        zt4 = [p.tile(f"s_z{c}", [128, BW], BF16) for c in range(NC8)]
        qe = p.tile("s_qe", [128, BW], BF16)
        load_w2(0)
        for nb in range(NBLK):
            for c in range(NC8):
                t, b = zt4[c]
                self.load(t[:, :], z3_h[128 * c:128 * (c + 1), nb * BW:(nb + 1) * BW], b, rbuf=z3_db)
            for ob in range(NC8):
                ps, ps_b = self.pb(4 + ob % 2)
                for c in range(NC8):
                    p.emit("tensor", lambda e, c=c, ob=ob, ps=ps: e.matmul(
                        ps[:, :], wql[0][:, 1024 * c + 128 * ob:1024 * c + 128 * (ob + 1)],
                        zt4[c][0][:, :], start=(c == 0), stop=(c == NC8 - 1), skip_group_check=True),
                        reads=[zt4[c][1], wql[1]], writes=[ps_b])
                p.emit("scalar", lambda e, ob=ob, ps=ps: e.activation(
                    qe[0][:, :], ps[:, :], AF.Identity, bias=bias4[:, ob:ob + 1]),
                    reads=[ps_b, bias4_b], writes=[qe[1]])
                self.store(q_h[128 * ob:128 * (ob + 1), nb * BW:(nb + 1) * BW], qe[0][:, :], qe[1], wbuf=q_b)
        load_w2(1)
        for kb in range(8):
            r = kb // 4
            for c in range(NC8):
                t, b = zt4[c]
                self.load(t[:, :], z3f_h[1024 * r + 128 * c:1024 * r + 128 * (c + 1),
                                         (kb % 4) * BW:(kb % 4 + 1) * BW], b, rbuf=z3f_b)
            for ob in range(NC8):
                ps, ps_b = self.pb(4 + ob % 2)
                for c in range(NC8):
                    p.emit("tensor", lambda e, c=c, ob=ob, ps=ps: e.matmul(
                        ps[:, :], wql[0][:, 1024 * c + 128 * ob:1024 * c + 128 * (ob + 1)],
                        zt4[c][0][:, :], start=(c == 0), stop=(c == NC8 - 1), skip_group_check=True),
                        reads=[zt4[c][1], wql[1]], writes=[ps_b])
                p.emit("scalar", lambda e, ob=ob, ps=ps: e.activation(
                    qe[0][:, :], ps[:, :], AF.Identity, bias=bias4[:, 8 + ob:9 + ob]),
                    reads=[ps_b, bias4_b], writes=[qe[1]])
                self.store(k_h[128 * ob:128 * (ob + 1), kb * BW:(kb + 1) * BW], qe[0][:, :], qe[1], wbuf=k_b)
        load_w2(2)
        vt_sb = p.tile("s_vt", [128, 1040], BF16)
        for kb in range(8):
            r = kb // 4
            for c in range(NC8):
                t, b = zt4[c]
                self.load(t[:, :], z3f_h[1024 * r + 128 * c:1024 * r + 128 * (c + 1),
                                         (kb % 4) * BW:(kb % 4 + 1) * BW], b, rbuf=z3f_b)
            for sub in range(4):
                p.emit("vector", lambda e: e.memset(vt_sb[0][:, :], 1.0), reads=[], writes=[vt_sb[1]])
                for vb in range(2):
                    ps, ps_b = self.pb(4 + vb)
                    for c in range(NC8):
                        t, b = zt4[c]
                        p.emit("tensor", lambda e, c=c, sub=sub, vb=vb, ps=ps, t=t, b=b: e.matmul(
                            ps[:, :], t[:, sub * 128:(sub + 1) * 128],
                            wql[0][:, 1024 * c + 512 * vb:1024 * c + 512 * (vb + 1)],
                            start=(c == 0), stop=False, skip_group_check=True),
                            reads=[b, wql[1]], writes=[ps_b])
                    p.emit("tensor", lambda e, vb=vb, ps=ps: e.matmul(
                        ps[:, :], ones_row16[0][0:1, :], bvr[0][0:1, 512 * vb:512 * (vb + 1)],
                        start=False, stop=True, skip_group_check=True),
                        reads=[bvr[1], ones_row16[1]], writes=[ps_b])
                    p.emit("vector", lambda e, vb=vb, ps=ps: e.tensor_copy(
                        vt_sb[0][:, 65 * 8 * vb:65 * 8 * (vb + 1)].rearrange("p (h d) -> p h d", d=65)[:, :, 0:64],
                        ps[:, :].rearrange("p (h d) -> p h d", d=64)),
                        reads=[ps_b], writes=[vt_sb[1]])
                self.store(v_h[kb * BW + sub * 128:kb * BW + (sub + 1) * 128, :], vt_sb[0][:, :], vt_sb[1], wbuf=v_b)

        self.p.new_phase()
        m0_h, m0_db = self.d("l2_m0")
        m1_h, m1_db = self.d("l2_m1")
        m0 = p.tile("s_m0", [128, 4 * BW], BF16)
        m1 = p.tile("s_m1", [128, 4 * BW], BF16)
        for kc in range(4):
            self.load(m0[0][:, kc * BW:(kc + 1) * BW], m0_h[kc, :, :], m0[1], rbuf=m0_db)
            self.load(m1[0][:, kc * BW:(kc + 1) * BW], m1_h[kc, :, :], m1[1], rbuf=m1_db)
        kpair = p.tile("s_kp", [128, 4096], BF16)
        vpair = p.tile("s_vp", [128, 32 * 130], BF16)
        qpair = p.tile("s_qp", [128, T], BF16)
        pe_t = p.tile("s_pe", [128, BW], BF16)
        po_t = p.tile("s_po", [128, BW], BF16)
        rec = p.tile("s_rec", [1, BW], F32)
        ytmp = p.tile("s_ytmp", [128, BW], F32)
        ctx = p.tile("s_ctx", [128, 8 * T], BF16)
        for hp in range(8):
            self.load(kpair[0][:, :], k_h[128 * hp:128 * (hp + 1), :], kpair[1], rbuf=k_b)
            for tk in range(32):
                self.load(vpair[0][:, 130 * tk:130 * (tk + 1)],
                          v_h[128 * tk:128 * (tk + 1), 130 * hp:130 * (hp + 1)], vpair[1], rbuf=v_b)
            self.load(qpair[0][:, :], q_h[128 * hp:128 * (hp + 1), :], qpair[1], rbuf=q_b)
            for i in range(4):
                blocks = list(range(0, i + 1)) + list(range(4, 4 + i + 1))
                nchunks = len(blocks) * 4
                ci = 0
                for gb in blocks:
                    for kc in range(4):
                        kcg = 4 * gb + kc
                        for par, ptl, bank in ((0, pe_t, 0), (1, po_t, 1)):
                            ps, ps_b = self.pb(bank)
                            p.emit("tensor", lambda e, par=par, kcg=kcg, i=i, ps=ps: e.matmul(
                                ps[:, :], kpair[0][64 * par:64 * (par + 1), 128 * kcg:128 * (kcg + 1)],
                                qpair[0][64 * par:64 * (par + 1), BW * i:BW * (i + 1)],
                                start=True, stop=True, skip_group_check=True,
                                tile_position=(64 * par, 0)),
                                reads=[kpair[1], qpair[1]], writes=[ps_b])
                            p.emit("scalar", lambda e, ptl=ptl, ps=ps: e.activation(
                                ptl[0][:, :], ps[:, :], AF.Relu, scale=self.cap(0.125)),
                                reads=[ps_b, cvb], writes=[ptl[1]])
                            if gb == i:
                                p.emit("vector", lambda e, ptl=ptl, kc=kc: e.tensor_tensor(
                                    ptl[0][:, :], ptl[0][:, :], m0[0][:, kc * BW:(kc + 1) * BW], ALU.mult),
                                    reads=[m0[1]], writes=[ptl[1]])
                            elif gb == 4 + i:
                                p.emit("vector", lambda e, ptl=ptl, kc=kc: e.tensor_tensor(
                                    ptl[0][:, :], ptl[0][:, :], m1[0][:, kc * BW:(kc + 1) * BW], ALU.mult),
                                    reads=[m1[1]], writes=[ptl[1]])
                            pv, pv_b = self.pb(2 + par)
                            p.emit("tensor", lambda e, par=par, kcg=kcg, ci=ci, ptl=ptl, pv=pv, nchunks=nchunks: e.matmul(
                                pv[0:65, :], vpair[0][:, 130 * kcg + 65 * par:130 * kcg + 65 * (par + 1)],
                                ptl[0][:, :], start=(ci == 0), stop=(ci == nchunks - 1), skip_group_check=True),
                                reads=[vpair[1], ptl[1]], writes=[pv_b])
                        ci += 1
                for par in range(2):
                    h = 2 * hp + par
                    pv, pv_b = self.pb(2 + par)
                    p.emit("vector", lambda e, pv=pv: e.tensor_scalar(
                        rec[0][0:1, :], pv[64:65, :], self.cap(1e-9, 1), None, ALU.add),
                        reads=[pv_b, cvb], writes=[rec[1]])
                    p.emit("vector", lambda e: e.reciprocal(rec[0][0:1, :], rec[0][0:1, :]),
                           reads=[], writes=[rec[1]])
                    ps2, ps2_b = self.pb(4)
                    p.emit("tensor", lambda e, ps2=ps2: e.matmul(
                        ps2[0:64, :], ones_row[0:1, 0:64], rec[0][0:1, :],
                        start=True, stop=True, skip_group_check=True),
                        reads=[rec[1], ones_row_b], writes=[ps2_b])
                    p.emit("vector", lambda e, pv=pv: e.tensor_copy(ytmp[0][0:64, :], pv[0:64, :]),
                           reads=[pv_b], writes=[ytmp[1]])
                    p.emit("vector", lambda e, h=h, i=i, ps2=ps2: e.tensor_tensor(
                        ctx[0][64 * (h % 2):64 * (h % 2) + 64, (h // 2) * T + BW * i:(h // 2) * T + BW * (i + 1)],
                        ytmp[0][0:64, :], ps2[0:64, :], ALU.mult),
                        reads=[ytmp[1], ps2_b], writes=[ctx[1]])

        wo_h, wo_db = self.d("l2_wo_t")
        wo, wo_b = self.load_weight_bf16("l2_wo_t", 1024, "wB")
        bias5, bias5_b = p.tile("l2_bias_sb", [128, 48], F32)
        self.load(bias5[:, :], bias_h[:, :], bias5_b, rbuf=bias_db)
        sout = p.tile("s_sout", [128, BW], F32)
        xl = p.tile("s_xl", [128, BW], F32)
        for i in range(4):
            for ob in range(8):
                ps, ps_b = self.pb(4 + ob % 2)
                for c in range(NC8):
                    p.emit("tensor", lambda e, c=c, ob=ob, i=i, ps=ps: e.matmul(
                        ps[:, :], wo[:, c * 1024 + 128 * ob:c * 1024 + 128 * (ob + 1)],
                        ctx[0][:, c * T + BW * i:c * T + BW * (i + 1)],
                        start=(c == 0), stop=(c == NC8 - 1), skip_group_check=True),
                        reads=[ctx[1], wo_b], writes=[ps_b])
                p.emit("scalar", lambda e, ob=ob, ps=ps: e.activation(
                    sout[0][:, :], ps[:, :], AF.Identity, bias=bias5[:, 24 + ob:25 + ob]),
                    reads=[ps_b, bias5_b], writes=[sout[1]])
                self.load(xl[0][:, :], z2_h[128 * ob:128 * (ob + 1), BW * i:BW * (i + 1)], xl[1], rbuf=z2_b)
                p.emit("vector", lambda e: e.tensor_tensor(sout[0][:, :], sout[0][:, :], xl[0][:, :], ALU.add),
                       reads=[xl[1]], writes=[sout[1]])
                self.store(z4_h[128 * ob:128 * (ob + 1), BW * i:BW * (i + 1)], sout[0][:, :], sout[1], wbuf=z4_b)

        self.p.new_phase()
        lp = lnp()
        self.load(lp[0][:, :], lp_h[:, :], lp[1], rbuf=lp_db)
        self.emit_lna("l2_z4", "l2_z5", lp, 32, 40, pfx="lna")
        self.p.new_phase()
        bias6, bias6_b = p.tile("l2_bias_sb", [128, 48], F32)
        self.load(bias6[:, :], bias_h[:, :], bias6_b, rbuf=bias_db)
        f1b_h, f1b_db = self.d("l2_ff1b")
        f1b = p.tile("s_f1b", [128, 32], F32)
        self.load(f1b[0][:, :], f1b_h[:, :], f1b[1], rbuf=f1b_db)
        ff1_h, ff1_db = self.d("l2_ff1_t")
        ff2_h, ff2_db = self.d("l2_ff2_t")
        z5_h, z5_db = self.d("l2_z5")
        w1sl = p.tile("s_w1sl", [128, 8 * 128], BF16)
        w1ss = p.tile("s_w1ss", [128, 8 * 128], F32)
        w2row = p.tile("s_w2row", [128, 1024], BF16)
        w2rs = p.tile("s_w2rs", [128, 1024], F32)
        usb = p.tile("s_usb", [128, 32 * BW], BF16)
        zt6 = [p.tile(f"s_z{c}", [128, BW], BF16) for c in range(NC8)]
        e1 = p.tile("s_e1", [128, BW], F32)
        gl = p.tile("s_gl", [128, BW], F32)
        xl6 = p.tile("s_xl", [128, BW], F32)
        for tokb in range(NBLK):
            for c in range(NC8):
                t, b = zt6[c]
                self.load(t[:, :], z5_h[128 * c:128 * (c + 1), tokb * BW:(tokb + 1) * BW], b, rbuf=z5_db)
            for ft in range(32):
                self.load(w1ss[0][:, :],
                          AP(ff1_h, 128 * ft, [[4096, 128], [128 * 4096, 8], [1, 128]]),
                          w1ss[1], rbuf=ff1_db)
                p.emit("vector", lambda e: e.tensor_copy(w1sl[0][:, :], w1ss[0][:, :]),
                       reads=[w1ss[1]], writes=[w1sl[1]])
                ps, ps_b = self.pb(6 + ft % 2)
                for c in range(NC8):
                    p.emit("tensor", lambda e, c=c, ps=ps: e.matmul(
                        ps[:, :], w1sl[0][:, 128 * c:128 * (c + 1)], zt6[c][0][:, :],
                        start=(c == 0), stop=(c == NC8 - 1), skip_group_check=True),
                        reads=[w1sl[1], zt6[c][1]], writes=[ps_b])
                p.emit("scalar", lambda e, ft=ft, ps=ps: e.activation(
                    usb[0][:, BW * ft:BW * (ft + 1)], ps[:, :], AF.Relu, bias=f1b[0][:, ft:ft + 1]),
                    reads=[ps_b, f1b[1]], writes=[usb[1]])
            ybanks = [self.pb(d2) for d2 in range(8)]
            for ft in range(32):
                self.load(w2rs[0][:, :], ff2_h[128 * ft:128 * (ft + 1), :], w2rs[1], rbuf=ff2_db)
                p.emit("vector", lambda e: e.tensor_copy(w2row[0][:, :], w2rs[0][:, :]),
                       reads=[w2rs[1]], writes=[w2row[1]])
                for d2 in range(8):
                    ps, ps_b = ybanks[d2]
                    p.emit("tensor", lambda e, d2=d2, ft=ft, ps=ps: e.matmul(
                        ps[:, :], w2row[0][:, 128 * d2:128 * (d2 + 1)], usb[0][:, BW * ft:BW * (ft + 1)],
                        start=(ft == 0), stop=(ft == 31), skip_group_check=True),
                        reads=[w2row[1], usb[1]], writes=[ps_b])
            for d2 in range(8):
                ps, ps_b = ybanks[d2]
                p.emit("scalar", lambda e, d2=d2, ps=ps: e.activation(
                    gl[0][:, :], ps[:, :], AF.Identity, bias=bias6[:, 40 + d2:41 + d2]),
                    reads=[ps_b, bias6_b], writes=[gl[1]])
                self.load(xl6[0][:, :], z4_h[128 * d2:128 * (d2 + 1), tokb * BW:(tokb + 1) * BW], xl6[1], rbuf=z4_b)
                p.emit("vector", lambda e: e.tensor_tensor(gl[0][:, :], gl[0][:, :], xl6[0][:, :], ALU.add),
                       reads=[xl6[1]], writes=[gl[1]])
                self.load(e1[0][:, :], g_h[128 * d2:128 * (d2 + 1), tokb * BW:(tokb + 1) * BW], e1[1], rbuf=g_b)
                p.emit("vector", lambda e: e.tensor_tensor(gl[0][:, :], gl[0][:, :], e1[0][:, :], ALU.mult),
                       reads=[e1[1]], writes=[gl[1]])
                self.load(xl6[0][:, :], x_h[128 * d2:128 * (d2 + 1), tokb * BW:(tokb + 1) * BW], xl6[1], rbuf=x_b)
                p.emit("vector", lambda e: e.tensor_tensor(gl[0][:, :], gl[0][:, :], xl6[0][:, :], ALU.add),
                       reads=[xl6[1]], writes=[gl[1]])
                self.store(x3_h[128 * d2:128 * (d2 + 1), tokb * BW:(tokb + 1) * BW], gl[0][:, :], gl[1], wbuf=x3_b)

    def emit_l3_attn(self, xn16_name, out_name):
        p = self.p
        self.p.new_phase()
        xn_h, xn_b = self.d(xn16_name)
        out_h, out_b = self.d(out_name)
        xf_h, xf_b = self.d("l3_xfull")
        qe_h, qe_b = self.d("l3_qe")
        qh_h, qh_b = self.d("l3_qh")
        ke_h, ke_b = self.d("l3_ke")
        kh_h, kh_b = self.d("l3_kh")
        v_h, v_b = self.d("l3_v")
        al_h, al_b = self.d("l3_alpha")
        sp_h, sp_b = self.d("l3_spike")
        ones_row, ones_row_b = p.tile("ones_row", [1, 128], F32)
        cvb = self.cvec_buf()

        p.emit("gpsimd", lambda e: e.collective_compute(
            "AllGather", ALU.bypass, PAIRS, [xn_h.ap().opt()], [xf_h.ap().opt()]),
            reads=[xn_b], writes=[xf_b], kind="coll")

        par_h, par_db = self.d("l3_par")
        par = p.tile("t_par", [1, 32], F32)
        self.load(par[0][0:1, 0:18], par_h[:, :], par[1], rbuf=par_db)
        p.emit("scalar", lambda e: e.activation(par[0][0:1, 18:19], par[0][0:1, 16:17], AF.Exp,
                                                scale=self.cap(-1.0, 1)),
               reads=[cvb], writes=[par[1]])
        p.emit("vector", lambda e: e.tensor_scalar(par[0][0:1, 18:19], par[0][0:1, 18:19], 1.0, None, ALU.add),
               reads=[], writes=[par[1]])
        p.emit("vector", lambda e: e.reciprocal(par[0][0:1, 18:19], par[0][0:1, 18:19]),
               reads=[], writes=[par[1]])
        p.emit("vector", lambda e: e.tensor_scalar(par[0][0:1, 18:19], par[0][0:1, 18:19], 1.5, None, ALU.mult),
               reads=[], writes=[par[1]])
        p.emit("vector", lambda e: e.tensor_tensor(par[0][0:1, 19:20], par[0][0:1, 18:19], par[0][0:1, 18:19], ALU.mult),
               reads=[], writes=[par[1]])
        kw = p.tile("t_kw", [1, 64], F32)
        p.emit("scalar", lambda e: e.activation(kw[0][0:1, 0:16], par[0][0:1, 0:16], AF.Abs),
               reads=[par[1]], writes=[kw[1]])
        p.emit("scalar", lambda e: e.activation(kw[0][0:1, 16:32], kw[0][0:1, 0:16], AF.Exp, scale=self.cap(-1.0, 1)),
               reads=[cvb], writes=[kw[1]])
        p.emit("scalar", lambda e: e.activation(kw[0][0:1, 16:32], kw[0][0:1, 16:32], AF.Ln, bias=self.cap(1.0, 1)),
               reads=[cvb], writes=[kw[1]])
        p.emit("scalar", lambda e: e.activation(kw[0][0:1, 32:48], par[0][0:1, 0:16], AF.Relu),
               reads=[par[1]], writes=[kw[1]])
        p.emit("vector", lambda e: e.tensor_tensor(kw[0][0:1, 16:32], kw[0][0:1, 16:32], kw[0][0:1, 32:48], ALU.add),
               reads=[], writes=[kw[1]])
        p.emit("vector", lambda e: e.tensor_scalar(kw[0][0:1, 16:32], kw[0][0:1, 16:32], 1e-6, None, ALU.add),
               reads=[], writes=[kw[1]])
        p.emit("scalar", lambda e: e.activation(kw[0][0:1, 16:32], kw[0][0:1, 16:32], AF.Ln),
               reads=[], writes=[kw[1]])
        p.emit("scalar", lambda e: e.activation(kw[0][0:1, 16:32], kw[0][0:1, 16:32], AF.Exp, scale=self.cap(-0.5, 1)),
               reads=[cvb], writes=[kw[1]])
        kd_h, kd_db = self.d("l3_kscd")
        self.store(kd_h[0:1, :], kw[0][0:1, 16:32], kw[1], wbuf=kd_db)
        ksc = p.tile("t_ksc", [128, 16], F32)
        ssb = p.tile("t_ssb", [128, 2], F32)
        psk, psk_b = self.pb(6)
        p.emit("tensor", lambda e: e.matmul(psk[:, 0:16], ones_row[0:1, :], kw[0][0:1, 16:32],
                                            start=True, stop=True, skip_group_check=True),
               reads=[kw[1], ones_row_b], writes=[psk_b])
        p.emit("vector", lambda e: e.tensor_copy(ksc[0][:, :], psk[:, 0:16]), reads=[psk_b], writes=[ksc[1]])
        p.emit("tensor", lambda e: e.matmul(psk[:, 0:2], ones_row[0:1, :], par[0][0:1, 18:20],
                                            start=True, stop=True, skip_group_check=True),
               reads=[par[1], ones_row_b], writes=[psk_b])
        p.emit("vector", lambda e: e.tensor_copy(ssb[0][:, :], psk[:, 0:2]), reads=[psk_b], writes=[ssb[1]])

        wia_h, wia_db = self.d("l3_wia")
        bia_h, bia_db = self.d("l3_bia")
        wia = p.tile("t_wia", [128, 8 * 17], BF16)
        wiaf = p.tile("t_wiaf", [128, 8 * 17], F32)
        wias = p.tile("t_wias", [128, 17], F32)
        bia_im, bia_im_b = p.tile("t_bia_im", [1, 1], F32)
        bia_al, bia_al_b = p.tile("t_bia_al", [16, 1], F32)
        self.load(bia_im[:, :], bia_h[0:1, :], bia_im_b, rbuf=bia_db)
        self.load(bia_al[:, :], bia_h[1:17, :], bia_al_b, rbuf=bia_db)
        for c in range(NC8):
            self.load(wias[0][:, :], wia_h[128 * c:128 * (c + 1), :], wias[1], rbuf=wia_db)
            p.emit("vector", lambda e, c=c: e.tensor_copy(wia[0][:, 17 * c:17 * (c + 1)], wias[0][:, :]),
                   reads=[wias[1]], writes=[wia[1]])
            p.emit("vector", lambda e, c=c: e.tensor_copy(wiaf[0][:, 17 * c:17 * (c + 1)], wias[0][:, :]),
                   reads=[wias[1]], writes=[wiaf[1]])
        xt3 = [p.tile(f"t_x{c}", [128, BW], BF16) for c in range(NC8)]
        xt32 = [p.tile(f"t_xf{c}", [128, BW], F32) for c in range(NC8)]
        xn32_h, xn32_db = self.d("xn32")
        ia = p.tile("t_ia", [16, BW], F32)
        im = p.tile("t_im", [1, BW], F32)
        for nb in range(NBLK):
            for c in range(NC8):
                t, b = xt3[c]
                self.load(t[:, :], xn_h[128 * c:128 * (c + 1), nb * BW:(nb + 1) * BW], b, rbuf=xn_b)
                t2, b2 = xt32[c]
                self.load(t2[:, :], xn32_h[128 * c:128 * (c + 1), nb * BW:(nb + 1) * BW], b2, rbuf=xn32_db)
            ps, ps_b = self.pb(0)
            for c in range(NC8):
                p.emit("tensor", lambda e, c=c, ps=ps: e.matmul(
                    ps[0:16, :], wiaf[0][:, 17 * c + 1:17 * (c + 1)], xt32[c][0][:, :],
                    start=(c == 0), stop=(c == NC8 - 1), skip_group_check=True),
                    reads=[xt32[c][1], wiaf[1]], writes=[ps_b])
            psi, psi_b = self.pb(1)
            for c in range(NC8):
                p.emit("tensor", lambda e, c=c, psi=psi: e.matmul(
                    psi[0:1, :], wiaf[0][:, 17 * c:17 * c + 1], xt32[c][0][:, :],
                    start=(c == 0), stop=(c == NC8 - 1), skip_group_check=True),
                    reads=[xt32[c][1], wiaf[1]], writes=[psi_b])
            p.emit("scalar", lambda e, psi=psi: e.activation(im[0][0:1, :], psi[0:1, :], AF.Identity,
                                                             bias=bia_im[0:1, :]),
                   reads=[psi_b, bia_im_b], writes=[im[1]])
            p.emit("scalar", lambda e, ps=ps: e.activation(ia[0][0:16, :], ps[0:16, :], AF.Identity,
                                                           bias=bia_al[0:16, :]),
                   reads=[ps_b, bia_al_b], writes=[ia[1]])
            for tl, nparts in ((im, 1), (ia, 16)):
                p.emit("scalar", lambda e, tl=tl, nparts=nparts: e.activation(
                    tl[0][0:nparts, :], tl[0][0:nparts, :], AF.Exp, scale=self.cap(-1.0, nparts)),
                    reads=[cvb], writes=[tl[1]])
                p.emit("vector", lambda e, tl=tl, nparts=nparts: e.tensor_scalar(
                    tl[0][0:nparts, :], tl[0][0:nparts, :], 1.0, None, ALU.add),
                    reads=[], writes=[tl[1]])
                p.emit("vector", lambda e, tl=tl, nparts=nparts: e.reciprocal(
                    tl[0][0:nparts, :], tl[0][0:nparts, :]),
                    reads=[], writes=[tl[1]])
            p.emit("vector", lambda e: e.tensor_scalar(im[0][0:1, :], im[0][0:1, :], par[0][0:1, 17:18], None, ALU.is_gt),
                   reads=[par[1]], writes=[im[1]])
            self.store(sp_h[0:1, nb * BW:(nb + 1) * BW], im[0][0:1, :], im[1], wbuf=sp_b)
            self.store(al_h[0:16, nb * BW:(nb + 1) * BW], ia[0][0:16, :], ia[1], wbuf=al_b)

        wq3, wq3_b = self.load_weight_bf16("l3_wqkv_t", 3072, "wA")
        bq_h, bq_db = self.d("l3_bqkv_cm")
        bq3, bq3_b = p.tile("t_bq", [128, 24], F32)
        self.load(bq3[:, :], bq_h[:, :], bq3_b, rbuf=bq_db)
        bvr_h, bvr_db = self.d("l3_bv_row")
        bvr = p.tile("t_bvr", [1, 1024], BF16)
        bvrs = p.tile("t_bvrs", [1, 1024], F32)
        self.load(bvrs[0][0:1, :], bvr_h[:, :], bvrs[1], rbuf=bvr_db)
        p.emit("vector", lambda e: e.tensor_copy(bvr[0][0:1, :], bvrs[0][0:1, :]), reads=[bvrs[1]], writes=[bvr[1]])
        ones_row16 = p.tile("t_ones16", [1, 128], BF16)
        p.emit("vector", lambda e: e.memset(ones_row16[0][0:1, :], 1.0), reads=[], writes=[ones_row16[1]])
        hsel = p.tile("t_hsel", [1, 128], F32)
        p.emit("vector", lambda e: e.memset(hsel[0][0:1, 0:64], -1.0), reads=[], writes=[hsel[1]])
        p.emit("vector", lambda e: e.memset(hsel[0][0:1, 0:1], 0.0), reads=[], writes=[hsel[1]])
        p.emit("vector", lambda e: e.memset(hsel[0][0:1, 64:128], 1.0), reads=[], writes=[hsel[1]])
        p.emit("vector", lambda e: e.memset(hsel[0][0:1, 64:65], 0.0), reads=[], writes=[hsel[1]])
        hs2 = p.tile("t_hs2", [128, 2], F32)
        p.emit("vector", lambda e: e.memset(hs2[0][:, :], 0.0), reads=[], writes=[hs2[1]])
        p.emit("vector", lambda e: e.memset(hs2[0][0:64, 0:1], 1.0), reads=[], writes=[hs2[1]])
        p.emit("vector", lambda e: e.memset(hs2[0][64:128, 1:2], 1.0), reads=[], writes=[hs2[1]])
        hs2T = p.tile("t_hs2T", [2, 128], F32)
        ident3, ident3_b = p.tile("ident_sb", [128, 128], F32)
        psT, psT_b = self.pb(6)
        p.emit("tensor", lambda e: e.transpose(psT[0:2, 0:128], hs2[0][:, :], ident3[:, :]),
               reads=[hs2[1], ident3_b], writes=[psT_b])
        p.emit("vector", lambda e: e.tensor_copy(hs2T[0][0:2, :], psT[0:2, 0:128]),
               reads=[psT_b], writes=[hs2T[1]])

        qf = p.tile("t_qf", [128, BW], F32)
        sqt = p.tile("t_sq", [128, BW], F32)
        nrm = p.tile("t_nrm", [2, BW], F32)
        qn_t = p.tile("t_qn", [128, BW], F32)
        sm1l = [p.tile(f"t_sm1{par2}", [1, 6 * BW], F32) for par2 in range(2)]
        ev16 = p.tile("t_ev16", [128, BW], BF16)
        hy16 = p.tile("t_hy16", [128, BW], BF16)

        def qkv_chunk(rhs_list, col0, dst_e_ap, dst_e_b, dst_h_ap, dst_h_b, tab_col, is_q):
            ps, ps_b = self.pb(4)
            for c in range(NC8):
                rhs, rhs_b = rhs_list[c]
                p.emit("tensor", lambda e, c=c, ps=ps, rhs=rhs, rhs_b=rhs_b: e.matmul(
                    ps[:, :], wq3[:, c * 3072 + col0:c * 3072 + col0 + 128],
                    rhs, start=(c == 0), stop=(c == NC8 - 1), skip_group_check=True),
                    reads=[rhs_b, wq3_b], writes=[ps_b])
            p.emit("scalar", lambda e, ps=ps, tab_col=tab_col: e.activation(
                qf[0][:, :], ps[:, :], AF.Identity, bias=bq3[:, tab_col:tab_col + 1]),
                reads=[ps_b, bq3_b], writes=[qf[1]])
            p.emit("vector", lambda e, is_q=is_q: e.tensor_scalar(
                ev16[0][:, :], qf[0][:, :], 0.125 if is_q else 1.0, None, ALU.mult),
                reads=[qf[1]], writes=[ev16[1]])
            self.store(dst_e_ap, ev16[0][:, :], ev16[1], wbuf=dst_e_b)
            p.emit("scalar", lambda e: e.activation(sqt[0][:, :], qf[0][:, :], AF.Square),
                   reads=[qf[1]], writes=[sqt[1]])
            psn, psn_b = self.pb(5)
            p.emit("tensor", lambda e, psn=psn: e.matmul(psn[0:2, :], hs2[0][:, :], sqt[0][:, :],
                                                         start=True, stop=True, skip_group_check=True),
                   reads=[sqt[1], hs2[1]], writes=[psn_b])
            p.emit("scalar", lambda e, psn=psn: e.activation(nrm[0][0:2, :], psn[0:2, :], AF.Ln),
                   reads=[psn_b], writes=[nrm[1]])
            p.emit("scalar", lambda e: e.activation(nrm[0][0:2, :], nrm[0][0:2, :], AF.Exp,
                                                    scale=self.cap(0.5, 2)),
                   reads=[cvb], writes=[nrm[1]])
            p.emit("vector", lambda e: e.tensor_scalar(nrm[0][0:2, :], nrm[0][0:2, :], 1e-12, None, ALU.max),
                   reads=[], writes=[nrm[1]])
            p.emit("vector", lambda e: e.reciprocal(nrm[0][0:2, :], nrm[0][0:2, :]),
                   reads=[], writes=[nrm[1]])
            psb2, psb2_b = self.pb(5)
            p.emit("tensor", lambda e, psb2=psb2: e.matmul(psb2[:, :], hs2T[0][0:2, :], nrm[0][0:2, :],
                                                           start=True, stop=True, skip_group_check=True),
                   reads=[nrm[1], hs2T[1]], writes=[psb2_b])
            p.emit("vector", lambda e, psb2=psb2: e.tensor_tensor(qn_t[0][:, :], qf[0][:, :], psb2[:, :], ALU.mult),
                   reads=[qf[1], psb2_b], writes=[qn_t[1]])
            p.emit("vector", lambda e: e.tensor_scalar(qn_t[0][:, :], qn_t[0][:, :], ssb[0][:, 0:1], None, ALU.mult),
                   reads=[ssb[1]], writes=[qn_t[1]])
            for par2 in range(2):
                base = 64 * par2
                sm1 = sm1l[par2]
                r0 = sm1[0][0:1, 0:BW]
                rm = sm1[0][0:1, BW:2 * BW]
                rsh = sm1[0][0:1, 2 * BW:3 * BW]
                rtime = sm1[0][0:1, 3 * BW:4 * BW]
                rw = sm1[0][0:1, 4 * BW:5 * BW]
                p.emit("scalar", lambda e, r0=r0, base=base: e.activation(r0, qn_t[0][base:base + 1, :], AF.Square),
                       reads=[qn_t[1]], writes=[sm1[1]])
                p.emit("vector", lambda e, r0=r0, rm=rm: e.tensor_scalar(
                    rm, r0, -2.0, par[0][0:1, 19:20], ALU.mult, ALU.add),
                    reads=[par[1]], writes=[sm1[1]])
                p.emit("vector", lambda e, rm=rm: e.tensor_scalar(rm, rm, 1e-8, None, ALU.max),
                       reads=[], writes=[sm1[1]])
                p.emit("scalar", lambda e, rm=rm: e.activation(rm, rm, AF.Ln),
                       reads=[], writes=[sm1[1]])
                p.emit("scalar", lambda e, rm=rm: e.activation(rm, rm, AF.Exp, scale=self.cap(0.5, 1)),
                       reads=[cvb], writes=[sm1[1]])
                p.emit("scalar", lambda e, rm=rm, rsh=rsh: e.activation(rsh, rm, AF.Exp),
                       reads=[], writes=[sm1[1]])
                p.emit("vector", lambda e, rsh=rsh, rw=rw: e.reciprocal(rw, rsh),
                       reads=[], writes=[sm1[1]])
                p.emit("vector", lambda e, rsh=rsh, rw=rw: e.tensor_tensor(rsh, rsh, rw, ALU.subtract),
                       reads=[], writes=[sm1[1]])
                p.emit("vector", lambda e, rm=rm, rw=rw: e.reciprocal(rw, rm),
                       reads=[], writes=[sm1[1]])
                p.emit("vector", lambda e, rsh=rsh, rw=rw: e.tensor_tensor(rsh, rsh, rw, ALU.mult),
                       reads=[], writes=[sm1[1]])
                p.emit("vector", lambda e, rsh=rsh: e.tensor_scalar(rsh, rsh, 0.5, None, ALU.mult),
                       reads=[], writes=[sm1[1]])
                p.emit("vector", lambda e, r0=r0, rtime=rtime: e.tensor_scalar(
                    rtime, r0, -1.0, par[0][0:1, 19:20], ALU.mult, ALU.add),
                    reads=[par[1]], writes=[sm1[1]])
                p.emit("vector", lambda e, rsh=rsh, rw=rw: e.tensor_tensor(rw, rsh, rsh, ALU.mult),
                       reads=[], writes=[sm1[1]])
                p.emit("vector", lambda e, rtime=rtime, rw=rw: e.tensor_tensor(rtime, rtime, rw, ALU.mult),
                       reads=[], writes=[sm1[1]])
                p.emit("scalar", lambda e, rtime=rtime: e.activation(rtime, rtime, AF.Ln, bias=self.cap(1.0, 1)),
                       reads=[cvb], writes=[sm1[1]])
                p.emit("scalar", lambda e, rtime=rtime: e.activation(rtime, rtime, AF.Exp, scale=self.cap(0.5, 1)),
                       reads=[cvb], writes=[sm1[1]])
                psb3, psb3_b = self.pb(5)
                hcol = 0 if is_q else 64
                p.emit("tensor", lambda e, psb3=psb3, hcol=hcol, rsh=rsh: e.matmul(
                    psb3[0:64, :], hsel[0][0:1, hcol:hcol + 64], rsh,
                    start=True, stop=True, skip_group_check=True),
                    reads=[sm1[1], hsel[1]], writes=[psb3_b])
                p.emit("vector", lambda e, psb3=psb3, base=base: e.tensor_tensor(
                    hy16[0][base:base + 64, :], qn_t[0][base:base + 64, :], psb3[0:64, :], ALU.mult),
                    reads=[qn_t[1], psb3_b], writes=[hy16[1]])
                p.emit("vector", lambda e, base=base, rtime=rtime: e.tensor_copy(
                    hy16[0][base:base + 1, :], rtime),
                    reads=[sm1[1]], writes=[hy16[1]])
            self.store(dst_h_ap, hy16[0][:, :], hy16[1], wbuf=dst_h_b)

        for nb in range(NBLK):
            for c in range(NC8):
                t, b = xt3[c]
                self.load(t[:, :], xn_h[128 * c:128 * (c + 1), nb * BW:(nb + 1) * BW], b, rbuf=xn_b)
            rl = [(t[:, :], b) for (t, b) in xt3]
            for ch in range(8):
                qkv_chunk(rl, 128 * ch,
                          qe_h[128 * ch:128 * (ch + 1), nb * BW:(nb + 1) * BW], qe_b,
                          qh_h[128 * ch:128 * (ch + 1), nb * BW:(nb + 1) * BW], qh_b,
                          ch, True)
        xg3 = [p.tile(f"t_xg{c}", [128, BW], BF16) for c in range(NC8)]
        vt3 = p.tile("t_vt", [128, 1040], BF16)
        for kb in range(8):
            r = kb // 4
            for c in range(NC8):
                t, b = xg3[c]
                self.load(t[:, :], xf_h[1024 * r + 128 * c:1024 * r + 128 * (c + 1),
                                        (kb % 4) * BW:(kb % 4 + 1) * BW], b, rbuf=xf_b)
            rl = [(t[:, :], b) for (t, b) in xg3]
            for ch in range(8):
                qkv_chunk(rl, 1024 + 128 * ch,
                          ke_h[128 * ch:128 * (ch + 1), kb * BW:(kb + 1) * BW], ke_b,
                          kh_h[128 * ch:128 * (ch + 1), kb * BW:(kb + 1) * BW], kh_b,
                          8 + ch, False)
            for sub in range(4):
                p.emit("vector", lambda e: e.memset(vt3[0][:, :], 1.0), reads=[], writes=[vt3[1]])
                for vb in range(2):
                    ps, ps_b = self.pb(4)
                    for c in range(NC8):
                        t, b = xg3[c]
                        p.emit("tensor", lambda e, c=c, sub=sub, vb=vb, ps=ps, t=t, b=b: e.matmul(
                            ps[:, :], t[:, sub * 128:(sub + 1) * 128],
                            wq3[:, c * 3072 + 2048 + 512 * vb:c * 3072 + 2048 + 512 * (vb + 1)],
                            start=(c == 0), stop=False, skip_group_check=True),
                            reads=[b, wq3_b], writes=[ps_b])
                    p.emit("tensor", lambda e, vb=vb, ps=ps: e.matmul(
                        ps[:, :], ones_row16[0][0:1, :], bvr[0][0:1, 512 * vb:512 * (vb + 1)],
                        start=False, stop=True, skip_group_check=True),
                        reads=[bvr[1], ones_row16[1]], writes=[ps_b])
                    p.emit("vector", lambda e, vb=vb, ps=ps: e.tensor_copy(
                        vt3[0][:, 65 * 8 * vb:65 * 8 * (vb + 1)].rearrange("p (h d) -> p h d", d=65)[:, :, 0:64],
                        ps[:, :].rearrange("p (h d) -> p h d", d=64)),
                        reads=[ps_b], writes=[vt3[1]])
                self.store(v_h[kb * BW + sub * 128:kb * BW + (sub + 1) * 128, :], vt3[0][:, :], vt3[1], wbuf=v_b)

        self.p.new_phase()
        m0_h, m0_db = self.d("l3_m0")
        m1_h, m1_db = self.d("l3_m1")
        m0 = p.tile("u_m0", [128, 4 * BW], BF16)
        m1 = p.tile("u_m1", [128, 4 * BW], BF16)
        for kc in range(4):
            self.load(m0[0][:, kc * BW:(kc + 1) * BW], m0_h[kc, :, :], m0[1], rbuf=m0_db)
            self.load(m1[0][:, kc * BW:(kc + 1) * BW], m1_h[kc, :, :], m1[1], rbuf=m1_db)
        ksc = p.tile("t_ksc", [128, 16], F32)
        kep = p.tile("u_kep", [128, 4096], BF16)
        khp = p.tile("u_khp", [128, 4096], BF16)
        vpair = p.tile("u_vp", [128, 32 * 130], BF16)
        qep = p.tile("u_qep", [128, T], BF16)
        qhp = p.tile("u_qhp", [128, T], BF16)
        spr = p.tile("u_spr", [1, T], F32)
        alr = [p.tile(f"u_al{par2}", [1, T], F32) for par2 in range(2)]
        alb = [p.tile(f"u_alb{par2}", [128, BW], F32) for par2 in range(2)]
        scrA = [p.tile(f"u_scA{par2}", [128, BW], F32) for par2 in range(2)]
        scrB = [p.tile(f"u_scB{par2}", [128, BW], F32) for par2 in range(2)]
        scrC = [p.tile(f"u_scC{par2}", [128, BW], F32) for par2 in range(2)]
        ptl3 = [p.tile(f"u_pt{par2}", [128, BW], BF16) for par2 in range(2)]
        rec = p.tile("u_rec", [1, BW], F32)
        ytmp = p.tile("u_ytmp", [128, BW], F32)
        ctx = p.tile("u_ctx", [128, 8 * T], BF16)
        self.load(spr[0][0:1, :], sp_h[0:1, :], spr[1], rbuf=sp_b)
        ksc_l, ksc_lb = ksc
        kw2 = p.tile("u_kw2", [1, 16], F32)
        kd_h, kd_b2 = self.d("l3_kscd")
        self.load(kw2[0][0:1, :], kd_h[0:1, :], kw2[1], rbuf=kd_b2)
        psk2, psk2_b = self.pb(6)
        p.emit("tensor", lambda e: e.matmul(psk2[:, 0:16], ones_row[0:1, :], kw2[0][0:1, :],
                                            start=True, stop=True, skip_group_check=True),
               reads=[kw2[1], ones_row_b], writes=[psk2_b])
        p.emit("vector", lambda e: e.tensor_copy(ksc_l[:, :], psk2[:, 0:16]), reads=[psk2_b], writes=[ksc_lb])

        for hp in range(8):
            self.load(kep[0][:, :], ke_h[128 * hp:128 * (hp + 1), :], kep[1], rbuf=ke_b)
            self.load(khp[0][:, :], kh_h[128 * hp:128 * (hp + 1), :], khp[1], rbuf=kh_b)
            for tk in range(32):
                self.load(vpair[0][:, 130 * tk:130 * (tk + 1)],
                          v_h[128 * tk:128 * (tk + 1), 130 * hp:130 * (hp + 1)], vpair[1], rbuf=v_b)
            self.load(qep[0][:, :], qe_h[128 * hp:128 * (hp + 1), :], qep[1], rbuf=qe_b)
            self.load(qhp[0][:, :], qh_h[128 * hp:128 * (hp + 1), :], qhp[1], rbuf=qh_b)
            for par2 in range(2):
                self.load(alr[par2][0][0:1, :], al_h[2 * hp + par2:2 * hp + par2 + 1, :],
                          alr[par2][1], rbuf=al_b)
            for i in range(4):
                for par2 in range(2):
                    psb4, psb4_b = self.pb(6)
                    p.emit("tensor", lambda e, par2=par2, i=i, psb4=psb4: e.matmul(
                        psb4[:, :], ones_row[0:1, :], alr[par2][0][0:1, BW * i:BW * (i + 1)],
                        start=True, stop=True, skip_group_check=True),
                        reads=[alr[par2][1], ones_row_b], writes=[psb4_b])
                    p.emit("vector", lambda e, par2=par2, psb4=psb4: e.tensor_copy(alb[par2][0][:, :], psb4[:, :]),
                           reads=[psb4_b], writes=[alb[par2][1]])
                blocks = list(range(0, i + 1)) + list(range(4, 4 + i + 1))
                nchunks = len(blocks) * 4
                ci = 0
                for gb in blocks:
                    for kc in range(4):
                        kcg = 4 * gb + kc
                        for par2 in range(2):
                            h = 2 * hp + par2
                            base = 64 * par2
                            pse, pse_b = self.pb(0 + par2)
                            psj, psj_b = self.pb(2 + par2)
                            p.emit("tensor", lambda e, par2=par2, kcg=kcg, i=i, pse=pse, base=base: e.matmul(
                                pse[:, :], kep[0][base:base + 64, 128 * kcg:128 * (kcg + 1)],
                                qep[0][base:base + 64, BW * i:BW * (i + 1)],
                                start=True, stop=True, skip_group_check=True,
                                tile_position=(base, 0)),
                                reads=[kep[1], qep[1]], writes=[pse_b])
                            p.emit("tensor", lambda e, par2=par2, kcg=kcg, i=i, psj=psj, base=base: e.matmul(
                                psj[:, :], khp[0][base:base + 64, 128 * kcg:128 * (kcg + 1)],
                                qhp[0][base:base + 64, BW * i:BW * (i + 1)],
                                start=True, stop=True, skip_group_check=True,
                                tile_position=(base, 0)),
                                reads=[khp[1], qhp[1]], writes=[psj_b])
                            A = scrA[par2]
                            Bt = scrB[par2]
                            Ct = scrC[par2]
                            p.emit("vector", lambda e, psj=psj, A=A: e.tensor_scalar(
                                A[0][:, :], psj[:, :], 1.0 + 1e-6, None, ALU.max),
                                reads=[psj_b], writes=[A[1]])
                            p.emit("scalar", lambda e, A=A, Bt=Bt: e.activation(Bt[0][:, :], A[0][:, :], AF.Square),
                                   reads=[A[1]], writes=[Bt[1]])
                            p.emit("scalar", lambda e, Bt=Bt: e.activation(Bt[0][:, :], Bt[0][:, :], AF.Ln,
                                                                           bias=self.cap(-1.0)),
                                   reads=[cvb], writes=[Bt[1]])
                            p.emit("scalar", lambda e, Bt=Bt: e.activation(Bt[0][:, :], Bt[0][:, :], AF.Exp,
                                                                           scale=self.cap(0.5)),
                                   reads=[cvb], writes=[Bt[1]])
                            p.emit("vector", lambda e, A=A, Bt=Bt: e.tensor_tensor(
                                Bt[0][:, :], A[0][:, :], Bt[0][:, :], ALU.add),
                                reads=[A[1]], writes=[Bt[1]])
                            p.emit("scalar", lambda e, Bt=Bt: e.activation(Bt[0][:, :], Bt[0][:, :], AF.Ln),
                                   reads=[], writes=[Bt[1]])
                            p.emit("scalar", lambda e, Bt=Bt, h=h: e.activation(
                                Bt[0][:, :], Bt[0][:, :], AF.Square, scale=ksc_l[:, h:h + 1]),
                                reads=[ksc_lb], writes=[Bt[1]])
                            p.emit("vector", lambda e, Bt=Bt, Ct=Ct, pse=pse: e.tensor_tensor(
                                Ct[0][:, :], Bt[0][:, :], pse[:, :], ALU.add),
                                reads=[Bt[1], pse_b], writes=[Ct[1]])
                            p.emit("vector", lambda e, Ct=Ct, par2=par2: e.tensor_tensor(
                                Ct[0][:, :], Ct[0][:, :], alb[par2][0][:, :], ALU.mult),
                                reads=[alb[par2][1]], writes=[Ct[1]])
                            p.emit("vector", lambda e, Ct=Ct, pse=pse: e.tensor_tensor(
                                Ct[0][:, :], pse[:, :], Ct[0][:, :], ALU.subtract),
                                reads=[pse_b], writes=[Ct[1]])
                            p.emit("scalar", lambda e, Ct=Ct, par2=par2: e.activation(
                                ptl3[par2][0][:, :], Ct[0][:, :], AF.Exp),
                                reads=[Ct[1]], writes=[ptl3[par2][1]])
                            if gb == i:
                                p.emit("vector", lambda e, par2=par2, kc=kc: e.tensor_tensor(
                                    ptl3[par2][0][:, :], ptl3[par2][0][:, :], m0[0][:, kc * BW:(kc + 1) * BW], ALU.mult),
                                    reads=[m0[1]], writes=[ptl3[par2][1]])
                            elif gb == 4 + i:
                                p.emit("vector", lambda e, par2=par2, kc=kc: e.tensor_tensor(
                                    ptl3[par2][0][:, :], ptl3[par2][0][:, :], m1[0][:, kc * BW:(kc + 1) * BW], ALU.mult),
                                    reads=[m1[1]], writes=[ptl3[par2][1]])
                            pv, pv_b = self.pb(4 + par2)
                            p.emit("tensor", lambda e, par2=par2, kcg=kcg, ci=ci, pv=pv, nchunks=nchunks: e.matmul(
                                pv[0:65, :], vpair[0][:, 130 * kcg + 65 * par2:130 * kcg + 65 * (par2 + 1)],
                                ptl3[par2][0][:, :], start=(ci == 0), stop=(ci == nchunks - 1),
                                skip_group_check=True),
                                reads=[vpair[1], ptl3[par2][1]], writes=[pv_b])
                        ci += 1
                for par2 in range(2):
                    h = 2 * hp + par2
                    pv, pv_b = self.pb(4 + par2)
                    p.emit("vector", lambda e, pv=pv: e.reciprocal(rec[0][0:1, :], pv[64:65, :]),
                           reads=[pv_b], writes=[rec[1]])
                    p.emit("vector", lambda e, i=i: e.tensor_tensor(
                        rec[0][0:1, :], rec[0][0:1, :], spr[0][0:1, BW * i:BW * (i + 1)], ALU.mult),
                        reads=[spr[1]], writes=[rec[1]])
                    psb5, psb5_b = self.pb(6)
                    p.emit("tensor", lambda e, psb5=psb5: e.matmul(
                        psb5[0:64, :], ones_row[0:1, 0:64], rec[0][0:1, :],
                        start=True, stop=True, skip_group_check=True),
                        reads=[rec[1], ones_row_b], writes=[psb5_b])
                    p.emit("vector", lambda e, pv=pv: e.tensor_copy(ytmp[0][0:64, :], pv[0:64, :]),
                           reads=[pv_b], writes=[ytmp[1]])
                    p.emit("vector", lambda e, h=h, i=i, psb5=psb5: e.tensor_tensor(
                        ctx[0][64 * (h % 2):64 * (h % 2) + 64, (h // 2) * T + BW * i:(h // 2) * T + BW * (i + 1)],
                        ytmp[0][0:64, :], psb5[0:64, :], ALU.mult),
                        reads=[ytmp[1], psb5_b], writes=[ctx[1]])

        wo3, wo3_b = self.load_weight_bf16("l3_wout_t", 1024, "wB")
        bo_h, bo_db = self.d("l3_bout_cm")
        bo3, bo3_b = p.tile("u_bo", [128, 8], F32)
        self.load(bo3[:, :], bo_h[:, :], bo3_b, rbuf=bo_db)
        sout = p.tile("u_sout", [128, BW], F32)
        for i in range(4):
            for ob in range(8):
                ps, ps_b = self.pb(0 + ob % 2)
                for c in range(NC8):
                    p.emit("tensor", lambda e, c=c, ob=ob, i=i, ps=ps: e.matmul(
                        ps[:, :], wo3[:, c * 1024 + 128 * ob:c * 1024 + 128 * (ob + 1)],
                        ctx[0][:, c * T + BW * i:c * T + BW * (i + 1)],
                        start=(c == 0), stop=(c == NC8 - 1), skip_group_check=True),
                        reads=[ctx[1], wo3_b], writes=[ps_b])
                p.emit("scalar", lambda e, ob=ob, ps=ps: e.activation(
                    sout[0][:, :], ps[:, :], AF.Identity, bias=bo3[:, ob:ob + 1]),
                    reads=[ps_b, bo3_b], writes=[sout[1]])
                self.store(out_h[128 * ob:128 * (ob + 1), BW * i:BW * (i + 1)], sout[0][:, :], sout[1], wbuf=out_b)


    def emit_l3_ffn_and_gate(self, xn16_name, x_name, s_name, x4_name):
        p = self.p
        self.p.new_phase()
        xn_h, xn_b = self.d(xn16_name)
        x_h, x_b = self.d(x_name)
        s_h, s_b = self.d(s_name)
        x4_h, x4_b = self.d(x4_name)
        wg_h, wg_db = self.d("l3_wg_t")
        wu_h, wu_db = self.d("l3_wu_t")
        wd_h, wd_db = self.d("l3_wd_t")
        cvb = self.cvec_buf()
        INV127 = 1.0 / 127.0
        INV127_3 = 1.0 / (127.0 ** 3)

        xnr = p.tile("w_xn", [128, 8 * T], BF16)
        for c in range(NC8):
            self.load(xnr[0][:, c * T:(c + 1) * T], xn_h[128 * c:128 * (c + 1), :], xnr[1], rbuf=xn_b)
        wsl = p.tile("w_wsl", [128, 8 * 128], I32)
        wsl16 = p.tile("w_wsl16", [128, 8 * 128], BF16)
        wrow = p.tile("w_wrow", [128, 1024], I32)
        wrow16 = p.tile("w_wrow16", [128, 1024], BF16)
        usb = p.tile("w_usb", [128, 24 * BW], BF16)
        e1 = p.tile("w_e1", [128, BW], F32)
        h1 = p.tile("w_h1", [128, BW], F32)
        gl = p.tile("w_gl", [128, BW], F32)
        xl = p.tile("w_xl", [128, BW], F32)
        for tokb in range(NBLK):
            for ft in range(24):
                for which, (w_h2, w_db2, bank) in enumerate(((wg_h, wg_db, 0), (wu_h, wu_db, 1))):
                    self.load(wsl[0][:, :],
                              AP(w_h2, 128 * ft, [[3072, 128], [128 * 3072, 8], [1, 128]]),
                              wsl[1], rbuf=w_db2)
                    p.emit("vector", lambda e: e.tensor_copy(wsl16[0][:, :], wsl[0][:, :]),
                           reads=[wsl[1]], writes=[wsl16[1]])
                    ps, ps_b = self.pb(bank)
                    for c in range(NC8):
                        p.emit("tensor", lambda e, c=c, tokb=tokb, ps=ps: e.matmul(
                            ps[:, :], wsl16[0][:, 128 * c:128 * (c + 1)],
                            xnr[0][:, c * T + tokb * BW:c * T + (tokb + 1) * BW],
                            start=(c == 0), stop=(c == NC8 - 1), skip_group_check=True),
                            reads=[wsl16[1], xnr[1]], writes=[ps_b])
                psg, psg_b = self.pb(0)
                psu, psu_b = self.pb(1)
                p.emit("scalar", lambda e: e.activation(e1[0][:, :], psg[:, :], AF.Exp,
                                                        scale=self.cap(-INV127)),
                       reads=[psg_b, cvb], writes=[e1[1]])
                p.emit("vector", lambda e: e.tensor_scalar(e1[0][:, :], e1[0][:, :], 1.0, None, ALU.add),
                       reads=[], writes=[e1[1]])
                p.emit("vector", lambda e: e.reciprocal(e1[0][:, :], e1[0][:, :]),
                       reads=[], writes=[e1[1]])
                p.emit("vector", lambda e: e.tensor_tensor(h1[0][:, :], psg[:, :], e1[0][:, :], ALU.mult),
                       reads=[psg_b, e1[1]], writes=[h1[1]])
                p.emit("vector", lambda e, ft=ft: e.tensor_tensor(
                    usb[0][:, BW * ft:BW * (ft + 1)], h1[0][:, :], psu[:, :], ALU.mult),
                    reads=[h1[1], psu_b], writes=[usb[1]])
            ybanks3 = [self.pb(d2) for d2 in range(8)]
            for ft in range(24):
                self.load(wrow[0][:, :], wd_h[128 * ft:128 * (ft + 1), :], wrow[1], rbuf=wd_db)
                p.emit("vector", lambda e: e.tensor_copy(wrow16[0][:, :], wrow[0][:, :]),
                       reads=[wrow[1]], writes=[wrow16[1]])
                for d2 in range(8):
                    ps, ps_b = ybanks3[d2]
                    p.emit("tensor", lambda e, d2=d2, ft=ft, ps=ps: e.matmul(
                        ps[:, :], wrow16[0][:, 128 * d2:128 * (d2 + 1)], usb[0][:, BW * ft:BW * (ft + 1)],
                        start=(ft == 0), stop=(ft == 23), skip_group_check=True),
                        reads=[wrow16[1], usb[1]], writes=[ps_b])
            for d2 in range(8):
                ps, ps_b = ybanks3[d2]
                p.emit("scalar", lambda e, ps=ps: e.activation(e1[0][:, :], ps[:, :], AF.Exp,
                                                               scale=self.cap(-INV127_3)),
                       reads=[ps_b, cvb], writes=[e1[1]])
                p.emit("vector", lambda e: e.tensor_scalar(e1[0][:, :], e1[0][:, :], 1.0, None, ALU.add),
                       reads=[], writes=[e1[1]])
                p.emit("vector", lambda e: e.reciprocal(e1[0][:, :], e1[0][:, :]),
                       reads=[], writes=[e1[1]])
                p.emit("vector", lambda e, ps=ps: e.tensor_tensor(gl[0][:, :], ps[:, :], e1[0][:, :], ALU.mult),
                       reads=[ps_b, e1[1]], writes=[gl[1]])
                p.emit("vector", lambda e: e.tensor_scalar(gl[0][:, :], gl[0][:, :], INV127_3, None, ALU.mult),
                       reads=[], writes=[gl[1]])
                self.load(xl[0][:, :], s_h[128 * d2:128 * (d2 + 1), tokb * BW:(tokb + 1) * BW], xl[1], rbuf=s_b)
                p.emit("vector", lambda e: e.tensor_tensor(gl[0][:, :], gl[0][:, :], xl[0][:, :], ALU.mult),
                       reads=[xl[1]], writes=[gl[1]])
                self.load(xl[0][:, :], x_h[128 * d2:128 * (d2 + 1), tokb * BW:(tokb + 1) * BW], xl[1], rbuf=x_b)
                p.emit("vector", lambda e: e.tensor_tensor(gl[0][:, :], gl[0][:, :], xl[0][:, :], ALU.add),
                       reads=[xl[1]], writes=[gl[1]])
                self.store(x4_h[128 * d2:128 * (d2 + 1), tokb * BW:(tokb + 1) * BW], gl[0][:, :], gl[1], wbuf=x4_b)

    def decl_layer3(self):
        self.inp("l3_par", [1, 18], F32)
        self.inp("l3_wia", [D, 17], F32)
        self.inp("l3_bia", [17, 1], F32)
        self.inp("l3_wqkv_t", [D, 3072], F32)
        self.inp("l3_bqkv_cm", [128, 24], F32)
        self.inp("l3_bv_row", [1, D], F32)
        self.inp("l3_wout_t", [D, D], F32)
        self.inp("l3_bout_cm", [128, 8], F32)
        self.inp("l3_m0", [4, 128, BW], BF16)
        self.inp("l3_m1", [4, 128, BW], BF16)
        self.inp("l3_wg_t", [D, 3072], I32)
        self.inp("l3_wu_t", [D, 3072], I32)
        self.inp("l3_wd_t", [3072, D], I32)
        self.scratch("l3_xfull", [2 * D, T], BF16, coll=True)
        self.scratch("l3_qe", [D, T], BF16)
        self.scratch("l3_qh", [D, T], BF16)
        self.scratch("l3_ke", [D, 4096], BF16)
        self.scratch("l3_kh", [D, 4096], BF16)
        self.scratch("l3_v", [4096, 16 * 65], BF16)
        self.scratch("l3_alpha", [16, T], F32)
        self.scratch("l3_spike", [1, T], F32)
        self.scratch("l3_kscd", [1, 16], F32)
        self.scratch("l3_s", [D, T], F32)

    @staticmethod
    def host_layer3(inputs, core):
        b, j = divmod(core, 2)
        f32 = np.float32
        g = lambda n: np.asarray(inputs[n], dtype=f32)
        gi = lambda n: np.asarray(inputs[n], dtype=np.int32)
        cm = lambda v: np.ascontiguousarray(v.reshape(-1, 128).T)
        par = np.zeros((1, 18), dtype=f32)
        par[0, 0:16] = g("L3_attn_log_k")
        par[0, 16] = g("L3_attn_qk_scale")
        par[0, 17] = g("L3_attn_spike_threshold")
        wia = np.concatenate([g("L3_attn_Wimp").T, g("L3_attn_Walpha").T], axis=1)
        bia = np.concatenate([g("L3_attn_bimp"), g("L3_attn_balpha")]).reshape(17, 1)
        diag = np.zeros((4, 128, BW), dtype=np.float32)
        for kc in range(4):
            r = np.arange(128)[:, None]
            c = np.arange(BW)[None, :]
            diag[kc] = (c >= (128 * kc + r)).astype(np.float32)
        import ml_dtypes
        bf = lambda a: a.astype(ml_dtypes.bfloat16)
        m0 = bf(diag if j == 0 else np.ones_like(diag))
        m1 = bf(np.zeros_like(diag) if j == 0 else diag)
        bqkv = g("L3_attn_bqkv")
        return {
            "l3_par": par,
            "l3_wia": np.ascontiguousarray(wia),
            "l3_bia": bia,
            "l3_wqkv_t": np.ascontiguousarray(g("L3_attn_Wqkv").T),
            "l3_bqkv_cm": cm(bqkv),
            "l3_bv_row": bqkv[2048:3072][None, :],
            "l3_wout_t": np.ascontiguousarray(g("L3_attn_Wout").T),
            "l3_bout_cm": cm(g("L3_attn_bout")),
            "l3_m0": m0, "l3_m1": m1,
            "l3_wg_t": np.ascontiguousarray(gi("L3_ffn_w_gate").T),
            "l3_wu_t": np.ascontiguousarray(gi("L3_ffn_w_up").T),
            "l3_wd_t": np.ascontiguousarray(gi("L3_ffn_w_down").T),
        }

    def build_layer3_test(self):
        self.inp("x_fm", [D, T], F32)
        self.inp("cvec", [128, 64], F32)
        self.decl_layer3()
        self.scratch("xn32", [D, T], F32)
        self.scratch("xn16", [D, T], BF16)
        self.scratch("x4_fm", [D, T], F32)
        self.init_consts()
        self.emit_ln("x_fm", "xn32", "xn16")
        self.emit_l3_attn("xn16", "l3_s")
        self.emit_l3_ffn_and_gate("xn16", "x_fm", "l3_s", "x4_fm")
        self.flush_dumps()
        self.p.finalize()
        return self.nc

    def init_consts(self):
        p = self.p
        ones_col, ones_col_b = p.tile("ones_col", [128, 1], F32)
        ones_row, ones_row_b = p.tile("ones_row", [1, 128], F32)
        p.emit("vector", lambda e: e.memset(ones_col[:, :], 1.0), writes=[ones_col_b])
        o16, o16_b = p.tile("ones_col16", [128, 1], BF16)
        p.emit("vector", lambda e: e.memset(o16[:, :], 1.0), writes=[o16_b])
        p.emit("vector", lambda e: e.memset(ones_row[:, :], 1.0), writes=[ones_row_b])
        cv_h, cv_b = self.d("cvec")
        ct, ctb = p.tile("cvec_sb", [128, 64], F32)
        self.load(ct[:, :], cv_h[:, :], ctb, rbuf=cv_b)
        ident_h = self.nc.inline_tensor(np.eye(128, dtype=np.float32), "ident128")
        it, it_b = p.tile("ident_sb", [128, 128], F32)
        self.load(it[:, :], ident_h.ap(), it_b)

    def build_v0(self):
        self.inp("x_fm", [D, T], F32)
        self.inp("cvec", [128, 64], F32)
        self.scratch("xn32", [D, T], F32)
        self.scratch("xn16", [D, T], BF16)
        if "stdbg" in self.dump:
            self.scratch("stdbg", [NBLK, 3 * BW], F32)
        self.init_consts()
        self.emit_ln("x_fm", "xn32", "xn16")
        self.p.finalize()
        return self.nc

    def host_cvec(self):
        v = np.zeros((128, 64), dtype=np.float32)
        for i, c in enumerate(self.consts):
            v[:, i] = c
        return v

    def decl_layer1(self):
        self.inp("l1_wqkv_t", [D, 3072], F32)
        self.inp("l1_wout_t", [D, D], F32)
        self.inp("l1_pa", [1, 32], F32)
        self.inp("l1_freqs_rep", [1, 128], F32)
        self.inp("l1_pos_own", [1, T], F32)
        self.inp("l1_pos_all", [1, 4096], F32)
        self.inp("l1_dvec", [1, 7680], F32)
        self.inp("l1_wr_t", [D, 8], F32)
        self.inp("l1_br", [8, 1], F32)
        self.inp("l1_w1t", [8, D, 4096], F32)
        self.inp("l1_b1cm", [128, 256], F32)
        self.inp("l1_w2t", [8, 4096, D], F32)
        self.inp("l1_b2", [8, D], F32)
        self.scratch("l1_xfull", [2 * D, T], BF16, coll=True)
        self.scratch("l1_q", [D, T], BF16)
        self.scratch("l1_k", [D, 4096], BF16)
        self.scratch("l1_v", [4096, 1040], BF16)
        self.scratch("l1_ekb", [16, 7680], BF16)
        self.scratch("l1_s", [D, T], F32)
        self.scratch("l1_w1q", [8, D, 4096], BF16)
        self.scratch("l1_w2q", [8, 4096, D], BF16)
        self.scratch("l1_ht", [8, 4096, T], BF16)
        self.scratch("l1_mask", [8, T], F32)

    @staticmethod
    def host_layer1(inputs, core):
        b, j = divmod(core, 2)
        qkv_w = np.asarray(inputs["L1_attn_qkv_w"], dtype=np.float32)
        perm = []
        for h in range(16):
            perm.extend(range(64 * h, 64 * h + 32))
        for h in range(16):
            perm.extend(range(64 * h + 32, 64 * h + 64))
        perm_q = np.array(perm)
        col_perm = np.concatenate([perm_q, 1024 + perm_q, np.arange(2048, 3072)])
        wq_t = np.ascontiguousarray(qkv_w.T[:, col_perm])
        blocks = core_token_blocks(j)
        pos_own = np.concatenate([np.arange(blk * BW, (blk + 1) * BW) for blk in blocks]).astype(np.float32)[None, :]
        gorder = [0, 2, 4, 6, 1, 3, 5, 7]
        pos_all = np.concatenate([np.arange(blk * BW, (blk + 1) * BW) for blk in gorder]).astype(np.float32)[None, :]
        m = np.arange(7680, dtype=np.float32)
        dvec = np.abs(m - 3583.0 - 512.0 * j)[None, :]
        W1 = np.asarray(inputs["L1_ffn_W1"], dtype=np.float32)
        W2 = np.asarray(inputs["L1_ffn_W2"], dtype=np.float32)
        b1 = np.asarray(inputs["L1_ffn_b1"], dtype=np.float32)
        return {
            "l1_wqkv_t": wq_t,
            "l1_wout_t": np.ascontiguousarray(np.asarray(inputs["L1_attn_out_w"], dtype=np.float32).T),
            "l1_pa": np.concatenate([np.asarray(inputs["L1_attn_bias_p"], dtype=np.float32).reshape(-1),
                                     np.asarray(inputs["L1_attn_bias_a"], dtype=np.float32).reshape(-1)])[None, :],
            "l1_freqs_rep": np.tile(np.asarray(inputs["L1_attn_rope_freqs"], dtype=np.float32), 4)[None, :],
            "l1_pos_own": pos_own,
            "l1_pos_all": pos_all,
            "l1_dvec": dvec,
            "l1_wr_t": np.ascontiguousarray(np.asarray(inputs["L1_ffn_Wr"], dtype=np.float32).T),
            "l1_br": np.asarray(inputs["L1_ffn_br"], dtype=np.float32).reshape(8, 1),
            "l1_w1t": np.ascontiguousarray(W1.transpose(0, 2, 1)),
            "l1_b1cm": np.ascontiguousarray(b1.reshape(8, 32, 128).transpose(2, 0, 1).reshape(128, 256)),
            "l1_w2t": np.ascontiguousarray(W2.transpose(0, 2, 1)),
            "l1_b2": np.asarray(inputs["L1_ffn_b2"], dtype=np.float32),
        }

    def build_layer1_test(self):
        self.inp("x_fm", [D, T], F32)
        self.inp("cvec", [128, 64], F32)
        self.decl_layer1()
        self.scratch("xn32", [D, T], F32)
        self.scratch("xn16", [D, T], BF16)
        self.scratch("x2_fm", [D, T], F32)
        self.init_consts()
        self.emit_ln("x_fm", "xn32", "xn16")
        self.p.new_phase()
        self.emit_l1_attn("xn16", "l1_s")
        self.emit_l1_moe_and_gate("xn16", "x_fm", "l1_s", "x2_fm")
        self.flush_dumps()
        self.p.finalize()
        return self.nc

    def decl_layer2(self):
        self.inp("l2_lnp", [128, 48], F32)
        self.inp("l2_sfw_t", [D, D], F32)
        self.inp("l2_mask_t", [D, D], F32)
        self.inp("l2_bias", [128, 48], F32)
        self.inp("l2_wqkv_t", [3, D, D], F32)
        self.inp("l2_bv_row", [1, D], F32)
        self.inp("l2_wo_t", [D, D], F32)
        self.inp("l2_ff1_t", [D, 4096], F32)
        self.inp("l2_ff1b", [128, 32], F32)
        self.inp("l2_ff2_t", [4096, D], F32)
        self.inp("l2_m0", [4, 128, BW], BF16)
        self.inp("l2_m1", [4, 128, BW], BF16)
        self.inp("l2_ws1", [D, 256], F32)
        self.inp("l2_ws2", [256, 1], F32)
        self.inp("l2_bs", [128, 3], F32)
        self.inp("l2f_wqkvo", [4, D, D], F32)
        self.inp("l2_ltmask", [2, 128, 256], F32)
        self.inp("l2_ownsel", [256, 128], F32)
        self.scratch("l2_z1", [D, T], BF16)
        self.scratch("l2_z2", [D, T], F32)
        self.scratch("l2_z3", [D, T], BF16)
        self.scratch("l2_z3f", [2 * D, T], BF16, coll=True)
        self.scratch("l2_z4", [D, T], F32)
        self.scratch("l2_z5", [D, T], BF16)
        self.scratch("l2_q", [D, T], BF16)
        self.scratch("l2_k", [D, 4096], BF16)
        self.scratch("l2_v", [4096, 16 * 65], BF16)
        self.scratch("l2_g", [D, T], F32)
        self.scratch("l2_scd", [1, 128], F32, coll=True)
        self.scratch("l2_scf", [2, 128], F32, coll=True)
        self.scratch("l2_gpk", [1, 264192], BF16, coll=True)
        self.scratch("l2_gpkf", [2, 264192], BF16, coll=True)

    @staticmethod
    def host_layer2(inputs, core):
        b, j = divmod(core, 2)
        f32 = np.float32
        g = lambda n: np.asarray(inputs[n], dtype=f32)
        cm = lambda v: np.ascontiguousarray(v.reshape(-1, 128).T)
        lnp = np.concatenate([cm(g("L2_attn_g1")), cm(g("L2_attn_b1")),
                              cm(g("L2_attn_g2")), cm(g("L2_attn_b2")),
                              cm(g("L2_attn_g3")), cm(g("L2_attn_b3"))], axis=1)
        bias = np.concatenate([cm(g("L2_attn_bq")), cm(g("L2_attn_bk")), cm(g("L2_attn_bv")),
                               cm(g("L2_attn_bo")), cm(g("L2_attn_sf_b")), cm(g("L2_attn_ff2_b"))], axis=1)
        diag = np.zeros((4, 128, BW), dtype=np.float32)
        for kc in range(4):
            r = np.arange(128)[:, None]
            c = np.arange(BW)[None, :]
            diag[kc] = (c >= (128 * kc + r)).astype(np.float32)
        ones = np.ones_like(diag)
        zeros = np.zeros_like(diag)
        import ml_dtypes
        bf = lambda a: a.astype(ml_dtypes.bfloat16)
        m0 = bf(diag if j == 0 else ones)
        m1 = bf(zeros if j == 0 else diag)
        gl2loc = np.zeros(256, dtype=np.int64)
        gorder = [0, 2, 4, 6, 1, 3, 5, 7]
        gg = np.arange(256)
        glob = np.zeros(256, dtype=np.int64)
        for idx in range(256):
            r_, rest = divmod(idx, 128)
            s_, w_ = divmod(rest, 32)
            glob[idx] = (2 * s_ + r_) * 32 + w_
        lt = (glob[None, :] < glob[:, None]).astype(np.float32)
        ltm = np.stack([lt[0:128], lt[128:256]])
        osel = np.zeros((256, 128), dtype=np.float32)
        for gloc in range(128):
            osel[128 * j + gloc, gloc] = 1.0
        return {
            "l2_lnp": lnp,
            "l2_sfw_t": np.ascontiguousarray(g("L2_attn_sf_w").T),
            "l2_mask_t": np.ascontiguousarray(g("L2_attn_mask").T),
            "l2_bias": bias,
            "l2_wqkv_t": np.ascontiguousarray(np.stack([g("L2_attn_wq").T, g("L2_attn_wk").T, g("L2_attn_wv").T])),
            "l2_bv_row": g("L2_attn_bv")[None, :],
            "l2_wo_t": np.ascontiguousarray(g("L2_attn_wo").T),
            "l2_ff1_t": np.ascontiguousarray(g("L2_attn_ff1_w").T),
            "l2_ff1b": np.ascontiguousarray(g("L2_attn_ff1_b").reshape(32, 128).T),
            "l2_ff2_t": np.ascontiguousarray(g("L2_attn_ff2_w").T),
            "l2_m0": m0, "l2_m1": m1,
            "l2_ws1": g("L2_ffn_Ws1"),
            "l2_ws2": g("L2_ffn_Ws2"),
            "l2_bs": np.concatenate([np.ascontiguousarray(g("L2_ffn_bs1").reshape(2, 128).T),
                                     np.full((128, 1), g("L2_ffn_bs2")[0], dtype=f32)], axis=1),
            "l2f_wqkvo": np.ascontiguousarray(np.stack([g("L2_ffn_Wq"), g("L2_ffn_Wk"),
                                                        g("L2_ffn_Wv"), g("L2_ffn_Wo")])),
            "l2_ltmask": ltm,
            "l2_ownsel": osel,
        }

    def build_layer2_test(self):
        self.inp("x_fm", [D, T], F32)
        self.inp("cvec", [128, 64], F32)
        self.decl_layer2()
        self.scratch("xn32", [D, T], F32)
        self.scratch("xn16", [D, T], BF16)
        self.scratch("x3_fm", [D, T], F32)
        self.init_consts()
        self.emit_ln("x_fm", "xn32", "xn16")
        self.p.new_phase()
        self.emit_l2_gbranch("xn32")
        self.emit_l2_sbranch("xn32", "x_fm", "x3_fm")
        self.flush_dumps()
        self.p.finalize()
        return self.nc

    def build_full(self):
        self.inp("x_fm", [D, T], F32)
        self.inp("cvec", [128, 64], F32)
        self.inp("l0_wqkv_t", [D, 3072], F32)
        self.inp("l0_wproj_t", [D, D], F32)
        self.inp("l0_bproj", [128, 8], F32)
        self.inp("l0_wg", [2816, 1024], F32)
        self.inp("l0_wu", [2816, 1024], F32)
        self.inp("l0_wd", [1024, 2816], F32)
        self.scratch("l0_q", [D, T], BF16)
        self.scratch("l0_vkin", [129, D], F32, coll=True)
        self.scratch("l0_vkout", [129, D], F32, coll=True)
        self.scratch("l0_s", [D, T], F32)
        self.scratch("l0_wgq_t", [1024, 2816], BF16)
        self.scratch("l0_wuq_t", [1024, 2816], BF16)
        self.scratch("l0_wdq_t", [2816, 1024], BF16)
        self.scratch("x1_fm", [D, T], F32)
        self.decl_layer1()
        self.scratch("x2_fm", [D, T], F32)
        self.decl_layer2()
        self.scratch("x3_fm", [D, T], F32)
        self.decl_layer3()
        self.scratch("xn32", [D, T], F32)
        self.scratch("xn16", [D, T], BF16)
        self.out("out_fm", [D, T], F32)
        self.init_consts()
        self.emit_ln("x_fm", "xn32", "xn16")
        self.p.new_phase()
        self.emit_l0_attn("xn16", "l0_s")
        self.p.new_phase()
        self.emit_l0_ffn_and_gate("xn32", "x_fm", "l0_s", "x1_fm")
        self.p.new_phase()
        self.emit_ln("x1_fm", "xn32", "xn16")
        self.p.new_phase()
        self.emit_l1_attn("xn16", "l1_s")
        self.emit_l1_moe_and_gate("xn16", "x1_fm", "l1_s", "x2_fm")
        self.p.new_phase()
        self.emit_ln("x2_fm", "xn32", "xn16")
        self.p.new_phase()
        self.emit_l2_gbranch("xn32")
        self.emit_l2_sbranch("xn32", "x2_fm", "x3_fm")
        self.p.new_phase()
        self.emit_ln("x3_fm", "xn32", "xn16")
        self.emit_l3_attn("xn16", "l3_s")
        self.emit_l3_ffn_and_gate("xn16", "x3_fm", "l3_s", "out_fm")
        self.flush_dumps()
        self.p.finalize()
        return self.nc

    @staticmethod
    def host_layer0(inputs, core):
        f32 = np.float32
        g = lambda n: np.asarray(inputs[n], dtype=f32)
        return {
            "l0_wqkv_t": np.ascontiguousarray(g("L0_attn_W_qkv").T),
            "l0_wproj_t": np.ascontiguousarray(g("L0_attn_W_proj").T),
            "l0_bproj": np.ascontiguousarray(g("L0_attn_b_proj").reshape(8, 128).T),
            "l0_wg": g("L0_ffn_w_gate"),
            "l0_wu": g("L0_ffn_w_up"),
            "l0_wd": g("L0_ffn_w_down"),
        }

    def build_layer0(self):
        self.inp("x_fm", [D, T], F32)
        self.inp("cvec", [128, 64], F32)
        self.inp("l0_wqkv_t", [D, 3072], F32)
        self.inp("l0_wproj_t", [D, D], F32)
        self.inp("l0_bproj", [128, 8], F32)
        self.inp("l0_wg", [2816, 1024], F32)
        self.inp("l0_wu", [2816, 1024], F32)
        self.inp("l0_wd", [1024, 2816], F32)
        self.scratch("xn32", [D, T], F32)
        self.scratch("xn16", [D, T], BF16)
        self.scratch("l0_q", [D, T], BF16)
        self.scratch("l0_vkin", [129, D], F32, coll=True)
        self.scratch("l0_vkout", [129, D], F32, coll=True)
        self.scratch("l0_s", [D, T], F32)
        self.scratch("l0_wgq_t", [1024, 2816], BF16)
        self.scratch("l0_wuq_t", [1024, 2816], BF16)
        self.scratch("l0_wdq_t", [2816, 1024], BF16)
        self.scratch("x1_fm", [D, T], F32)
        self.init_consts()
        self.emit_ln("x_fm", "xn32", "xn16")
        self.p.new_phase()
        self.emit_l0_attn("xn16", "l0_s")
        self.p.new_phase()
        self.emit_l0_ffn_and_gate("xn32", "x_fm", "l0_s", "x1_fm")
        self.flush_dumps()
        self.p.finalize()
        return self.nc

    def build_l0a(self):
        self.inp("x_fm", [D, T], F32)
        self.inp("cvec", [128, 64], F32)
        self.inp("l0_wqkv_t", [D, 3072], F32)
        self.inp("l0_wproj_t", [D, D], F32)
        self.inp("l0_bproj", [128, 8], F32)
        self.scratch("xn32", [D, T], F32)
        self.scratch("xn16", [D, T], BF16)
        self.scratch("l0_q", [D, T], BF16)
        self.scratch("l0_vkin", [129, D], F32, coll=True)
        self.scratch("l0_vkout", [129, D], F32, coll=True)
        self.scratch("l0_s", [D, T], F32)
        if "l0_kvdbg" in self.dump:
            self.scratch("l0_kvdbg", [2, T, D], BF16)
        self.init_consts()
        self.emit_ln("x_fm", "xn32", "xn16")
        self.emit_l0_attn("xn16", "l0_s")
        self.flush_dumps()
        self.p.finalize()
        return self.nc


def core_token_blocks(j):
    return [2 * i + j for i in range(NBLK)]


def shard_x(x):
    shards = []
    for c in range(8):
        b, j = divmod(c, 2)
        blocks = core_token_blocks(j)
        rows = np.concatenate([x[b, blk * BW:(blk + 1) * BW, :] for blk in blocks], axis=0)
        shards.append(np.ascontiguousarray(rows.T))
    return shards


def unshard_out(outs):
    x = np.zeros((B, S, D), dtype=outs[0].dtype)
    for c in range(8):
        b, j = divmod(c, 2)
        rows = outs[c].T
        for i, blk in enumerate(core_token_blocks(j)):
            x[b, blk * BW:(blk + 1) * BW, :] = rows[i * BW:(i + 1) * BW]
    return x


def kernel(**inputs):
    mk = MK()
    nc = mk.build_full()
    xs = shard_x(np.asarray(inputs["L0_attn_x"], dtype=np.float32))
    cv = mk.host_cvec()
    base0 = MK.host_layer0(inputs, 0)
    in_maps = []
    for c in range(8):
        m = dict(base0)
        m.update(MK.host_layer1(inputs, c))
        m.update(MK.host_layer2(inputs, c))
        m.update(MK.host_layer3(inputs, c))
        m["x_fm"] = xs[c]
        m["cvec"] = cv
        in_maps.append(m)
    res = run_bass_kernel_spmd(nc, in_maps, core_ids=list(range(8)))
    return unshard_out([res.results[c]["out_fm"] for c in range(8)])
